# version 09
# baseline (speedup 1.0000x reference)
import numpy as np
import ml_dtypes
from contextlib import ExitStack

import concourse.bass as bass
import concourse.mybir as mybir
from concourse.bass_utils import run_bass_kernel_spmd

F32 = mybir.dt.float32
BF16 = mybir.dt.bfloat16
AF = mybir.ActivationFunctionType
OP = mybir.AluOpType
AX = mybir.AxisListType

D = 2048
KC = 16
TCORE = 2048
MT = 1024
NMT = TCORE // MT
TSUB = MT // 512
TB = MT // 128
E = 32
FMC = 6
HC = 4
FS = 4096
CBW = 256
NCB = FS // CBW
DENOM = float(np.sqrt(np.float32(2048.0)) / 4096.0)
EPS = 1e-5


class S:
    def __init__(self, h):
        self.h = h
        self.n = 0

    def inc(self, k=1):
        self.n += k
        return self.n


class Plan:
    def __init__(self):
        self.items = []

    def load(self, sem, out_ap, in_ap, war=None):
        self.items.append((sem, out_ap, in_ap, list(war or [])))
        return sem.inc(16)


def build_nc(stop_after=None, debug=False, pref_ebs=KC):
    nc = bass.Bass("TRN2", target_bir_lowering=False)

    x_own = nc.dram_tensor("x_own", [TCORE, D], F32, kind="ExternalInput").ap()
    x_pre = nc.dram_tensor("x_pre", [TCORE, D], F32, kind="ExternalInput").ap()
    flag = nc.dram_tensor("flag", [128, 1], F32, kind="ExternalInput").ap()
    ident = nc.dram_tensor("ident", [128, 128], BF16, kind="ExternalInput").ap()
    identf = nc.dram_tensor("identf", [128, 128], F32, kind="ExternalInput").ap()
    wq_t = nc.dram_tensor("wq_t", [KC, 128, KC, 128], F32, kind="ExternalInput").ap()
    wk_t = nc.dram_tensor("wk_t", [KC, 128, KC, 128], F32, kind="ExternalInput").ap()
    wv_t = nc.dram_tensor("wv_t", [KC, 128, KC, 128], F32, kind="ExternalInput").ap()
    wo_t = nc.dram_tensor("wo_t", [4, 128, 4, D], F32, kind="ExternalInput").ap()
    rw_t = nc.dram_tensor("rw_t", [128, KC, E], F32, kind="ExternalInput").ap()
    sgw_t = nc.dram_tensor("sgw_t", [128, KC, 1], F32, kind="ExternalInput").ap()
    wg_t = nc.dram_tensor("wg_t", [E, FMC, 128, KC, 128], F32, kind="ExternalInput").ap()
    wu_t = nc.dram_tensor("wu_t", [E, FMC, 128, KC, 128], F32, kind="ExternalInput").ap()
    wd_t = nc.dram_tensor("wd_t", [E, HC, 128, FMC, 512], F32, kind="ExternalInput").ap()
    swg_t = nc.dram_tensor("swg_t", [NCB, 128, KC, CBW], F32, kind="ExternalInput").ap()
    swu_t = nc.dram_tensor("swu_t", [NCB, 128, KC, CBW], F32, kind="ExternalInput").ap()
    swd_t = nc.dram_tensor("swd_t", [NCB, 128, 2, D], F32, kind="ExternalInput").ap()
    out = nc.dram_tensor("out", [TCORE, D], F32, kind="ExternalOutput").ap()
    if debug:
        d_xn = nc.dram_tensor("d_xn", [128, KC, MT], BF16, kind="ExternalOutput").ap()
        d_carry = nc.dram_tensor("d_carry", [128, KC], F32, kind="ExternalOutput").ap()
        d_pref = nc.dram_tensor("d_pref", [128, KC], F32, kind="ExternalOutput").ap()
        d_comb = nc.dram_tensor("d_comb", [128, TB, E], F32, kind="ExternalOutput").ap()
        d_gt = nc.dram_tensor("d_gt", [128, TB], F32, kind="ExternalOutput").ap()
        d_acc = nc.dram_tensor("d_acc", [128, TB, D], F32, kind="ExternalOutput").ap()

    with ExitStack() as es:
        acc = es.enter_context(nc.sbuf_tensor("acc", [128, TB, D], F32))
        xn = es.enter_context(nc.sbuf_tensor("xn", [128, KC, MT], BF16))
        ones = es.enter_context(nc.sbuf_tensor("ones", [128, MT], F32))
        carry = es.enter_context(nc.sbuf_tensor("carry", [128, KC], F32))
        pref = es.enter_context(nc.sbuf_tensor("pref", [128, KC], F32))
        flag_sb = es.enter_context(nc.sbuf_tensor("flag_sb", [128, 1], F32))
        id_sb = es.enter_context(nc.sbuf_tensor("id_sb", [128, 128], BF16))
        idf_sb = es.enter_context(nc.sbuf_tensor("idf_sb", [128, 128], F32))
        ones1 = es.enter_context(nc.sbuf_tensor("ones1", [1, 128], F32))
        comb_fm = es.enter_context(nc.sbuf_tensor("comb_fm", [32, MT], F32))
        comb = es.enter_context(nc.sbuf_tensor("comb", [128, TB, E], F32))
        gt = es.enter_context(nc.sbuf_tensor("gt", [128, TB, 1], F32))
        xn_tm2 = es.enter_context(nc.sbuf_tensor("xn_tm2", [128, 2, D], BF16))
        s2a = es.enter_context(nc.sbuf_tensor("s2a", [128, 2], F32))
        s2b = es.enter_context(nc.sbuf_tensor("s2b", [128, 2], F32))
        s2c = es.enter_context(nc.sbuf_tensor("s2c", [128, 2], F32))
        s2d = es.enter_context(nc.sbuf_tensor("s2d", [128, 2], F32))
        cur = {}
        lg = es.enter_context(nc.sbuf_tensor("lg", [128, E], F32))
        eee = es.enter_context(nc.sbuf_tensor("eee", [128, E], F32))
        msk = es.enter_context(nc.sbuf_tensor("msk", [128, E], F32))
        top8 = es.enter_context(nc.sbuf_tensor("top8", [128, 8], F32))
        s_a = es.enter_context(nc.sbuf_tensor("s_a", [128, 1], F32))
        s_b = es.enter_context(nc.sbuf_tensor("s_b", [128, 1], F32))
        s_c = es.enter_context(nc.sbuf_tensor("s_c", [128, 1], F32))
        s_d = es.enter_context(nc.sbuf_tensor("s_d", [128, 1], F32))
        s_e = es.enter_context(nc.sbuf_tensor("s_e", [128, 1], F32))
        eps_sb = es.enter_context(nc.sbuf_tensor("eps_sb", [128, 1], F32))

        pe = S(es.enter_context(nc.semaphore("pe")))
        dv = S(es.enter_context(nc.semaphore("dv")))
        sc = S(es.enter_context(nc.semaphore("sc")))
        ws = [S(es.enter_context(nc.semaphore("w%d" % i))) for i in range(6)]
        qc = S(es.enter_context(nc.semaphore("qc")))
        xa = S(es.enter_context(nc.semaphore("xa")))
        xb = S(es.enter_context(nc.semaphore("xb")))
        qs = S(es.enter_context(nc.semaphore("qs")))
        dma_sems = ws + [qc, xa, xb, qs]

        block = es.enter_context(nc.Block())
        planw = Plan()
        planx = Plan()

        def emit_plan(q, plan):
            for dsem, out_ap, in_ap, war in plan.items:
                for (sem, val) in war:
                    if val > 0:
                        q.wait_ge(sem.h, val)
                q.dma_start(out=out_ap, in_=in_ap).then_inc(dsem.h, 16)
            plan.items = []

        def vsync(v):
            v.drain()
            v.sem_inc(dv.h, 1)
            dv.inc()
            v.wait_ge(dv.h, dv.n)

        def ssync(s):
            s.drain()
            s.sem_inc(sc.h, 1)
            sc.inc()
            s.wait_ge(sc.h, sc.n)

        def barrier():
            @block.gpsimd
            def _(g):
                emit_plan(g, planw)

            @block.sync
            def _(sy):
                emit_plan(sy, planx)

            @block.tensor
            def _(t):
                t.drain()
                t.sem_inc(pe.h, 1)
            pe.inc()

            @block.vector
            def _(v):
                v.drain()
                v.sem_inc(dv.h, 1)
            dv.inc()

            @block.scalar
            def _(s):
                s.drain()
                s.sem_inc(sc.h, 1)
            sc.inc()
            targets = ([(pe, pe.n), (dv, dv.n), (sc, sc.n)]
                       + [(s2, s2.n) for s2 in dma_sems])

            @block.tensor
            def _(t):
                for sem, val in targets:
                    if sem is not pe:
                        t.wait_ge(sem.h, val)

            @block.vector
            def _(v):
                for sem, val in targets:
                    if sem is not dv:
                        v.wait_ge(sem.h, val)

            @block.scalar
            def _(s):
                for sem, val in targets:
                    if sem is not sc:
                        s.wait_ge(sem.h, val)

            @block.gpsimd
            def _(g):
                for sem, val in targets:
                    g.wait_ge(sem.h, val)

            @block.sync
            def _(sy):
                for sem, val in targets:
                    sy.wait_ge(sem.h, val)

        def finish():
            @block.gpsimd
            def _(g):
                emit_plan(g, planw)

            @block.sync
            def _(sy):
                emit_plan(sy, planx)
                if debug:
                    sy.wait_ge(dv.h, dv.n)
                    sy.wait_ge(sc.h, sc.n)
                    sy.dma_start(out=d_xn, in_=xn[:]).then_inc(qs.h, 16)
                    sy.dma_start(out=d_carry, in_=carry[:]).then_inc(qs.h, 16)
                    sy.dma_start(out=d_pref, in_=pref[:]).then_inc(qs.h, 16)
                    sy.dma_start(out=d_comb, in_=comb[:]).then_inc(qs.h, 16)
                    sy.dma_start(out=d_gt, in_=gt.ap().rearrange("p a b -> p (a b)")).then_inc(qs.h, 16)
                    sy.dma_start(out=d_acc, in_=acc[:]).then_inc(qs.h, 16)
                    qs.inc(16 * 6)
                for s2 in dma_sems:
                    sy.wait_ge(s2.h, s2.n)

            @block.vector
            def _(v):
                for s2 in dma_sems:
                    v.wait_ge(s2.h, s2.n)

        def fence():
            @block.gpsimd
            def _(g):
                emit_plan(g, planw)

            @block.sync
            def _(sy):
                emit_plan(sy, planx)
            pn, dn, sn = pe.n, dv.n, sc.n

            @block.tensor
            def _(t):
                t.wait_ge(dv.h, dn)
                t.wait_ge(sc.h, sn)

            @block.vector
            def _(v):
                v.wait_ge(pe.h, pn)
                v.wait_ge(sc.h, sn)

            @block.scalar
            def _(s):
                s.wait_ge(pe.h, pn)
                s.wait_ge(dv.h, dn)

        planx.load(qc, id_sb[:], ident)
        planx.load(qc, idf_sb[:], identf)
        t_consts = planx.load(qc, flag_sb[:], flag)

        @block.vector
        def _(v):
            v.memset(ones[:], 1.0)
            v.memset(pref[:], 0.0)
            v.memset(carry[:], 0.0)
            v.memset(eps_sb[:], EPS)
            v.memset(ones1[:], 1.0)
            if debug:
                v.memset(acc[:], 0.0)
                v.memset(comb[:], 0.0)
                v.memset(gt[:], 0.0)
                v.memset(xn[:], 0.0)
            v.drain()
            v.sem_inc(dv.h, 1)
        dv.inc()

        if stop_after == "init":
            finish()
            return nc

        def ln_emit(chunks, psT, dst_fm, router=None):
            n = len(chunks)
            xn_tag = [0] * n; sq_tag = [0] * n; var_tag = [0] * n
            trlast = [0] * n; cplast = [0] * n
            cp_slot = [0, 0]
            rmm_tag = [0] * n; gmm_tag = [0] * n; lgcp = [0] * n; sig = [0] * n
            comb_tag = [0] * n; cfcp = [0] * n

            def chain(t):
                ch = chunks[t]
                src = ch["src"]
                if "load" in ch:
                    war = []
                    if t >= 2:
                        war = [(dv, xn_tag[t - 2]), (sc, sq_tag[t - 2])]
                    ready = ch["load"](war)
                else:
                    ready = ch["ready"]
                par = t % 2
                sa = s2a[:, par:par + 1]; sb = s2b[:, par:par + 1]
                sc_ = s2c[:, par:par + 1]; sd = s2d[:, par:par + 1]

                @block.vector
                def _(v):
                    for sem, val in ready:
                        if val > 0:
                            v.wait_ge(sem.h, val)
                    if t >= 2 and trlast[t - 2] > 0:
                        v.wait_ge(pe.h, trlast[t - 2])
                    v.tensor_reduce(sa, src, AX.X, OP.add)
                    vsync(v)

                @block.scalar
                def _(s):
                    for sem, val in ready:
                        if val > 0:
                            s.wait_ge(sem.h, val)
                    if t >= 2:
                        s.wait_ge(dv.h, var_tag[t - 2])
                    s.activation(cur["scr"][:], src, AF.Square,
                                 accum_out=sb).then_inc(sc.h, 1)
                sq_tag[t] = sc.inc()

                @block.vector
                def _(v):
                    v.wait_ge(sc.h, sq_tag[t])
                    v.tensor_scalar(sa, sa, 1.0 / D, None, OP.mult)
                    v.tensor_scalar(sb, sb, 1.0 / D, None, OP.mult)
                    vsync(v)
                    v.tensor_tensor(sc_, sa, sa, OP.mult)
                    vsync(v)
                    v.tensor_tensor(sc_, sb, sc_, OP.subtract)
                    v.drain()
                    v.sem_inc(dv.h, 1)
                var_tag[t] = dv.inc()

                @block.scalar
                def _(s):
                    s.wait_ge(dv.h, var_tag[t])
                    s.activation(sd, sc_, AF.Ln, bias=eps_sb[:, 0:1])
                    ssync(s)
                    s.activation(sd, sd, AF.Exp, scale=-0.5)
                    s.drain()
                    s.sem_inc(sc.h, 1)
                t_sd = sc.inc()

                @block.vector
                def _(v):
                    v.wait_ge(sc.h, t_sd)
                    if t >= 2 and trlast[t - 2] == 0:
                        pass
                    v.tensor_scalar(xn_tm2[:, par, :], src, sa, sd,
                                    OP.subtract, OP.mult)
                    v.drain()
                    v.sem_inc(dv.h, 1)
                xn_tag[t] = dv.inc()

            def trcopies(t):
                col0 = chunks[t]["col0"]
                par = t % 2
                for fb in range(KC):
                    @block.tensor
                    def _(t_, fb=fb):
                        if fb == 0:
                            t_.wait_ge(dv.h, xn_tag[t])
                            t_.wait_ge(qc.h, t_consts)
                        if cp_slot[fb % 2] > 0:
                            t_.wait_ge(dv.h, cp_slot[fb % 2])
                        t_.transpose(psT[:, fb % 2, 0:128],
                                     xn_tm2[:, par, fb * 128:(fb + 1) * 128],
                                     id_sb[:]).then_inc(pe.h, 1)
                    t_tr = pe.inc()

                    @block.vector
                    def _(v, fb=fb, t_tr=t_tr):
                        v.wait_ge(pe.h, t_tr)
                        v.tensor_copy(dst_fm[:, fb, col0:col0 + 128],
                                      psT[:, fb % 2, 0:128]).then_inc(dv.h, 1)
                    cp_slot[fb % 2] = dv.inc()
                trlast[t] = pe.n
                cplast[t] = dv.n

            def router_pe(t):
                r = router
                par = t % 2

                @block.tensor
                def _(t_):
                    if t == 0:
                        t_.wait_ge(ws[0].h, r["t_rw"])
                        t_.wait_ge(ws[2].h, r["t_sg"])
                    t_.wait_ge(dv.h, cplast[t])
                    if t >= 2:
                        t_.wait_ge(dv.h, lgcp[t - 2])
                    mm = None
                    for k in range(KC):
                        mm = t_.matmul(r["psR"][:, par, 0:E],
                                       xn[:, k, t * 128:(t + 1) * 128],
                                       r["rw_b"][:, k, :],
                                       start=(k == 0), stop=(k == KC - 1))
                    mm.then_inc(pe.h, 1)
                rmm_tag[t] = pe.inc()

                @block.tensor
                def _(t_):
                    if t >= 2:
                        t_.wait_ge(sc.h, sig[t - 2])
                    mm = None
                    for k in range(KC):
                        mm = t_.matmul(r["psGt"][:, par, 0:1],
                                       xn[:, k, t * 128:(t + 1) * 128],
                                       r["sg_b"][:, k, :],
                                       start=(k == 0), stop=(k == KC - 1))
                    mm.then_inc(pe.h, 1)
                gmm_tag[t] = pe.inc()

            def router_rest(t):
                r = router
                par = t % 2

                @block.scalar
                def _(s):
                    s.wait_ge(pe.h, gmm_tag[t])
                    if t >= 1:
                        s.wait_ge(dv.h, sig_dv[t - 1])
                    s.activation(s_e[:], r["psGt"][:, par, 0:1], AF.Exp,
                                 scale=-1.0).then_inc(sc.h, 1)
                sig[t] = sc.inc()

                @block.vector
                def _(v):
                    v.wait_ge(sc.h, sig[t])
                    v.tensor_scalar(gt[:, t, :], s_e[:], 1.0, None, OP.add)
                    vsync(v)
                    v.reciprocal(gt[:, t, :], gt[:, t, :])
                    v.drain()
                    v.sem_inc(dv.h, 1)
                sig_dv[t] = dv.inc()

                @block.vector
                def _(v):
                    v.wait_ge(pe.h, rmm_tag[t])
                    v.tensor_copy(lg[:], r["psR"][:, par, 0:E]).then_inc(dv.h, 1)
                    vsync(v)
                    v.tensor_reduce(s_a[:], lg[:], AX.X, OP.max)
                    vsync(v)
                    v.tensor_scalar(s_a[:], s_a[:], -1.0, None, OP.mult)
                    v.drain()
                    v.sem_inc(dv.h, 1)
                lgcp[t] = dv.inc()
                t_nm = dv.inc()

                @block.scalar
                def _(s):
                    s.wait_ge(dv.h, t_nm)
                    s.activation(eee[:], lg[:], AF.Exp, bias=s_a[:],
                                 scale=1.0).then_inc(sc.h, 1)
                t_ex = sc.inc()

                @block.vector
                def _(v):
                    v.wait_ge(sc.h, t_ex)
                    v.max(top8[:], eee[:])
                    vsync(v)
                    v.tensor_reduce(s_b[:], top8[:, 0:4], AX.X, OP.add)
                    vsync(v)
                    v.reciprocal(s_b[:], s_b[:])
                    vsync(v)
                    v.tensor_scalar(msk[:], eee[:], top8[:, 3:4], None, OP.is_ge)
                    vsync(v)
                    v.tensor_tensor(msk[:], eee[:], msk[:], OP.mult)
                    vsync(v)
                    v.tensor_scalar(comb[:, t, :], msk[:], s_b[:, 0:1], None, OP.mult)
                    v.drain()
                    v.sem_inc(dv.h, 1)
                comb_tag[t] = dv.inc()

            def cfT(t):
                r = router

                @block.tensor
                def _(t_):
                    t_.wait_ge(dv.h, comb_tag[t])
                    if t >= 2:
                        t_.wait_ge(dv.h, cfcp[t - 2])
                    t_.transpose(r["psCF"][0:E, t % 2, 0:128], comb[:, t, :],
                                 idf_sb[:]).then_inc(pe.h, 1)
                t_cf = pe.inc()

                @block.vector
                def _(v):
                    v.wait_ge(pe.h, t_cf)
                    v.tensor_copy(comb_fm[:, t * 128:(t + 1) * 128],
                                  r["psCF"][0:E, t % 2, 0:128]).then_inc(dv.h, 1)
                cfcp[t] = dv.inc()

            sig_dv = [0] * n
            for step in range(n + 3):
                if step < n:
                    chain(step)
                if 1 <= step and step - 1 < n:
                    trcopies(step - 1)
                if router is not None and 2 <= step and step - 2 < n:
                    router_pe(step - 2)
                    router_rest(step - 2)
                if router is not None and 3 <= step and step - 3 < n:
                    cfT(step - 3)
            return dv.n

        for half in range(2):
            with ExitStack() as ps:
                psT = ps.enter_context(nc.psum_tensor("psTp%d" % half, [128, 2, 2048], BF16))
                psK = ps.enter_context(nc.psum_tensor("psKp%d" % half, [128, 2, 512], F32))
                psV = ps.enter_context(nc.psum_tensor("psVp%d" % half, [128, 2, 512], F32))
                xld = ps.enter_context(nc.sbuf_tensor("xldp%d" % half, [128, 2, D], F32))
                cur["scr"] = ps.enter_context(nc.sbuf_tensor("scrp%d" % half, [128, D], F32))
                kscr = ps.enter_context(nc.sbuf_tensor("kscrp%d" % half, [128, MT], F32))
                kvscr = ps.enter_context(nc.sbuf_tensor("kvscrp%d" % half, [128, MT], F32))
                wk_b = ps.enter_context(nc.sbuf_tensor("wkp%d" % half, [128, 2, KC, 128], BF16))
                wv_b = ps.enter_context(nc.sbuf_tensor("wvp%d" % half, [128, 2, KC, 128], BF16))

                base_dv, base_sc = dv.n, sc.n

                def mkload(ch):
                    sl = ch % 2
                    xsem = xa if sl == 0 else xb
                    r0 = half * MT + ch * 128

                    def load(war):
                        w2 = [(dv, base_dv), (sc, base_sc)] if not war else war
                        t_x = planx.load(xsem, xld[:, sl, :],
                                         x_pre[r0:r0 + 128, :], war=w2)
                        return [(xsem, t_x)]
                    return load
                chunks = [dict(src=xld[:, ch % 2, :], col0=ch * 128,
                               load=mkload(ch)) for ch in range(TB)]
                ln_emit(chunks, psT, xn)
                xn_ready = dv.n
                if stop_after == "prefln%d" % half:
                    barrier()
                    finish()
                    return nc

                p0 = pe.n
                prev_use = {"k0": p0, "k1": p0, "v0": p0, "v1": p0}
                evK, evV = [0], [0]
                for eb in range(pref_ebs):
                    sl = eb % 2
                    wks = ws[0 + sl]
                    wvs = ws[2 + sl]
                    t_wk = planw.load(wks, wk_b[:, sl], wk_t[eb],
                                      war=[(pe, prev_use["k%d" % sl])])
                    t_wv = planw.load(wvs, wv_b[:, sl], wv_t[eb],
                                      war=[(pe, prev_use["v%d" % sl])])

                    @block.tensor
                    def _(t, sl=sl, t_wk=t_wk, eb=eb, wks=wks):
                        t.wait_ge(wks.h, t_wk)
                        if eb == 0:
                            t.wait_ge(dv.h, xn_ready)
                        if evK[0] > 0:
                            t.wait_ge(dv.h, evK[0])
                        for ts2 in range(TSUB):
                            for k in range(KC):
                                mm = t.matmul(psK[:, ts2, :], wk_b[:, sl, k, :],
                                         xn[:, k, ts2 * 512:(ts2 + 1) * 512],
                                         start=(k == 0), stop=(k == KC - 1))
                        mm.then_inc(pe.h, 1)
                    t_kmm = pe.inc()
                    prev_use["k%d" % sl] = t_kmm

                    @block.tensor
                    def _(t, sl=sl, t_wv=t_wv, wvs=wvs):
                        t.wait_ge(wvs.h, t_wv)
                        if evV[0] > 0:
                            t.wait_ge(dv.h, evV[0])
                        for ts2 in range(TSUB):
                            for k in range(KC):
                                mm = t.matmul(psV[:, ts2, :], wv_b[:, sl, k, :],
                                         xn[:, k, ts2 * 512:(ts2 + 1) * 512],
                                         start=(k == 0), stop=(k == KC - 1))
                        mm.then_inc(pe.h, 1)
                    t_vmm = pe.inc()
                    prev_use["v%d" % sl] = t_vmm

                    @block.vector
                    def _(v, t_kmm=t_kmm, t_vmm=t_vmm, eb=eb):
                        v.wait_ge(pe.h, t_kmm)
                        v.tensor_copy(kscr[:], psK.ap().rearrange("p a b -> p (a b)")).then_inc(dv.h, 1)
                    evK[0] = dv.inc()

                    @block.vector
                    def _(v, t_vmm=t_vmm, eb=eb, t_k=evK[0]):
                        v.wait_ge(pe.h, t_vmm)
                        v.wait_ge(dv.h, t_k)
                        v.tensor_tensor(kvscr[:], psV.ap().rearrange("p a b -> p (a b)"),
                                        kscr[:], OP.mult).then_inc(dv.h, 1)
                    evV[0] = dv.inc()

                    @block.vector
                    def _(v, eb=eb):
                        vsync(v)
                        v.tensor_reduce(s_e[:], kvscr[:], AX.X, OP.add)
                        vsync(v)
                        v.tensor_tensor(pref[:, eb:eb + 1], pref[:, eb:eb + 1],
                                        s_e[:], OP.add)
                barrier()
            if stop_after == "prefkv%d" % half:
                finish()
                return nc

        @block.vector
        def _(v):
            v.wait_ge(qc.h, t_consts)
            v.tensor_scalar(carry[:], pref[:], flag_sb[:, 0:1], None, OP.mult)
            v.drain()
            v.sem_inc(dv.h, 1)
        dv.inc()

        if stop_after == "prefix":
            finish()
            return nc

        store_tags = []
        for mt in range(NMT):
            tok0 = mt * MT

            with ExitStack() as ps:
                psT = ps.enter_context(nc.psum_tensor("psT%d" % mt, [128, 2, 1024], BF16))
                psR = ps.enter_context(nc.psum_tensor("psR%d" % mt, [128, 2, 512], F32))
                psCF = ps.enter_context(nc.psum_tensor("psCF%d" % mt, [128, 2, 512], F32))
                psG = ps.enter_context(nc.psum_tensor("psGt%d" % mt, [128, 2, 512], F32))
                cur["scr"] = ps.enter_context(nc.sbuf_tensor("scrl%d" % mt, [128, D], F32))
                rw_b = ps.enter_context(nc.sbuf_tensor("rw_b%d" % mt, [128, KC, E], BF16))
                sg_b = ps.enter_context(nc.sbuf_tensor("sg_b%d" % mt, [128, KC, 1], BF16))
                p0 = pe.n
                t_rw = planw.load(ws[0], rw_b[:], rw_t, war=[(pe, p0)])
                t_sg = planw.load(ws[2], sg_b[:], sgw_t, war=[(pe, p0)])
                war = []
                if store_tags:
                    war.append((qs, store_tags[-1]))
                t_x = planx.load(
                    qs, acc[:],
                    x_own[tok0:tok0 + MT, :].rearrange("(a p) d -> p a d", p=128),
                    war=war)
                router_ctx = dict(psR=psR, psGt=psG, psCF=psCF, rw_b=rw_b,
                                  sg_b=sg_b, t_rw=t_rw, t_sg=t_sg)
                chunks = [dict(src=acc[:, tb, :], col0=tb * 128,
                               ready=[(qs, t_x)]) for tb in range(TB)]
                ln_emit(chunks, psT, xn, router=router_ctx)
                fence()
            if stop_after == "ln%d" % mt:
                finish()
                return nc
            if stop_after == "router%d" % mt:
                finish()
                return nc

            with ExitStack() as ps:
                vhat = ps.enter_context(nc.sbuf_tensor("vhat%d" % mt, [128, KC, MT], BF16))
                vhat_tags = []
                with ExitStack() as ps2:
                    psK = ps2.enter_context(nc.psum_tensor("psK%d" % mt, [128, 2, 512], F32))
                    psV = ps2.enter_context(nc.psum_tensor("psV%d" % mt, [128, 2, 512], F32))
                    psQ = ps2.enter_context(nc.psum_tensor("psQ%d" % mt, [128, 2, 512], F32))
                    wq_b = ps2.enter_context(nc.sbuf_tensor("wqa%d" % mt, [128, 2, KC, 128], BF16))
                    wk_b = ps2.enter_context(nc.sbuf_tensor("wka%d" % mt, [128, 2, KC, 128], BF16))
                    wv_b = ps2.enter_context(nc.sbuf_tensor("wva%d" % mt, [128, 2, KC, 128], BF16))
                    kscr = ps2.enter_context(nc.sbuf_tensor("kscr%d" % mt, [128, MT], F32))
                    kvscr = ps2.enter_context(nc.sbuf_tensor("kvscr%d" % mt, [128, MT], F32))
                    kvtscr = ps2.enter_context(nc.sbuf_tensor("kvtscr%d" % mt, [128, MT], F32))
                    p0 = pe.n
                    prev_use = {"q0": p0, "q1": p0, "k0": p0, "k1": p0,
                                "v0": p0, "v1": p0}
                    evK, evV, evQ = [0], [0], [0]
                    for eb in range(KC):
                        sl = eb % 2
                        wqs, wks, wvs = ws[0 + sl], ws[2 + sl], ws[4 + sl]
                        t_wq = planw.load(wqs, wq_b[:, sl], wq_t[eb],
                                          war=[(pe, prev_use.get("q%d" % sl, 0))])
                        t_wk = planw.load(wks, wk_b[:, sl], wk_t[eb],
                                          war=[(pe, prev_use.get("k%d" % sl, 0))])
                        t_wv = planw.load(wvs, wv_b[:, sl], wv_t[eb],
                                          war=[(pe, prev_use.get("v%d" % sl, 0))])

                        @block.tensor
                        def _(t, sl=sl, t_wk=t_wk, wks=wks):
                            t.wait_ge(wks.h, t_wk)
                            if evK[0] > 0:
                                t.wait_ge(dv.h, evK[0])
                            for ts2 in range(TSUB):
                                for k in range(KC):
                                    mm = t.matmul(psK[:, ts2, :], wk_b[:, sl, k, :],
                                             xn[:, k, ts2 * 512:(ts2 + 1) * 512],
                                             start=(k == 0), stop=(k == KC - 1))
                            mm.then_inc(pe.h, 1)
                        t_kmm = pe.inc()
                        prev_use["k%d" % sl] = t_kmm

                        @block.tensor
                        def _(t, sl=sl, t_wv=t_wv, wvs=wvs):
                            t.wait_ge(wvs.h, t_wv)
                            if evV[0] > 0:
                                t.wait_ge(dv.h, evV[0])
                            for ts2 in range(TSUB):
                                for k in range(KC):
                                    mm = t.matmul(psV[:, ts2, :], wv_b[:, sl, k, :],
                                             xn[:, k, ts2 * 512:(ts2 + 1) * 512],
                                             start=(k == 0), stop=(k == KC - 1))
                            mm.then_inc(pe.h, 1)
                        t_vmm = pe.inc()
                        prev_use["v%d" % sl] = t_vmm

                        @block.tensor
                        def _(t, sl=sl, t_wq=t_wq, wqs=wqs):
                            t.wait_ge(wqs.h, t_wq)
                            if evQ[0] > 0:
                                t.wait_ge(dv.h, evQ[0])
                            for ts2 in range(TSUB):
                                for k in range(KC):
                                    mm = t.matmul(psQ[:, ts2, :], wq_b[:, sl, k, :],
                                             xn[:, k, ts2 * 512:(ts2 + 1) * 512],
                                             start=(k == 0), stop=(k == KC - 1))
                            mm.then_inc(pe.h, 1)
                        t_qmm = pe.inc()
                        prev_use["q%d" % sl] = t_qmm

                        @block.vector
                        def _(v, t_kmm=t_kmm):
                            v.wait_ge(pe.h, t_kmm)
                            v.tensor_copy(kscr[:], psK.ap().rearrange("p a b -> p (a b)")).then_inc(dv.h, 1)
                        evK[0] = dv.inc()

                        @block.vector
                        def _(v, t_vmm=t_vmm, t_k=evK[0]):
                            v.wait_ge(pe.h, t_vmm)
                            v.wait_ge(dv.h, t_k)
                            v.tensor_tensor(kvscr[:], psV.ap().rearrange("p a b -> p (a b)"),
                                            kscr[:], OP.mult).then_inc(dv.h, 1)
                        evV[0] = dv.inc()

                        @block.vector
                        def _(v, eb=eb):
                            vsync(v)
                            v.tensor_tensor_scan(kvtscr[:], ones[:], kvscr[:],
                                                 carry[:, eb:eb + 1], OP.mult, OP.add)
                            vsync(v)
                            v.tensor_copy(carry[:, eb:eb + 1], kvtscr[:, MT - 1:MT])
                        t_scan = dv.n

                        @block.vector
                        def _(v, eb=eb, t_qmm=t_qmm, t_scan=t_scan):
                            v.wait_ge(pe.h, t_qmm)
                            v.wait_ge(dv.h, t_scan)
                            v.scalar_tensor_tensor(vhat[:, eb, :],
                                                   psQ.ap().rearrange("p a b -> p (a b)"),
                                                   DENOM, kvtscr[:], OP.mult,
                                                   OP.mult).then_inc(dv.h, 1)
                        evQ[0] = dv.inc()
                        vhat_tags.append(evQ[0])
                    qkv_done_pe = pe.n

                with ExitStack() as ps2:
                    psO = ps2.enter_context(nc.psum_tensor("psO%d" % mt, [128, 4, 512], F32))
                    wo_b = ps2.enter_context(nc.sbuf_tensor("wo_b%d" % mt, [128, 2, 4, D], BF16))
                    prev_wo = [qkv_done_pe, qkv_done_pe]
                    prev_wo_dv = [vhat_tags[-1], vhat_tags[-1]]
                    ev_o = [0, 0, 0, 0]
                    prev_g_ev = [0]
                    for g in range(4):
                        sl = g % 2
                        t_wo = planw.load(ws[sl], wo_b[:, sl], wo_t[g],
                                          war=[(pe, prev_wo[sl]), (dv, prev_wo_dv[sl])])
                        prev_wo_dv[sl] = 0
                        for tb in range(TB):
                            for fc in range(HC):
                                idx = tb * HC + fc
                                psl = [0, 2, 1, 3][idx % 4]

                                @block.tensor
                                def _(t, g=g, sl=sl, tb=tb, fc=fc, psl=psl,
                                      t_wo=t_wo, idx=idx):
                                    if idx == 0:
                                        t.wait_ge(ws[sl].h, t_wo)
                                        t.wait_ge(dv.h, vhat_tags[-1] if g == 0
                                                  else vhat_tags[g * 4 + 3])
                                    if ev_o[psl] > 0:
                                        t.wait_ge(dv.h, ev_o[psl])
                                    mm = None
                                    for k in range(4):
                                        mm = t.matmul(psO[:, psl, :],
                                                 vhat[:, g * 4 + k, tb * 128:(tb + 1) * 128],
                                                 wo_b[:, sl, k, fc * 512:(fc + 1) * 512],
                                                 start=(k == 0), stop=(k == 3))
                                    mm.then_inc(pe.h, 1)
                                t_omm = pe.inc()

                                @block.vector
                                def _(v, tb=tb, fc=fc, psl=psl, t_omm=t_omm,
                                      idx=idx, pg=prev_g_ev[0]):
                                    v.wait_ge(pe.h, t_omm)
                                    if idx == 0 and pg > 0:
                                        v.wait_ge(dv.h, pg)
                                    v.tensor_tensor(acc[:, tb, fc * 512:(fc + 1) * 512],
                                                    psO[:, psl, :],
                                                    acc[:, tb, fc * 512:(fc + 1) * 512],
                                                    OP.add).then_inc(dv.h, 1)
                                ev_o[psl] = dv.inc()
                        prev_g_ev[0] = dv.n
                        prev_wo[sl] = pe.n
                    fence()
            if stop_after == "attn%d" % mt:
                finish()
                return nc

            with ExitStack() as ps:
                psGm = ps.enter_context(nc.psum_tensor("psGm%d" % mt, [128, 2, 512], F32))
                psUm = ps.enter_context(nc.psum_tensor("psUm%d" % mt, [128, 2, 512], F32))
                psDm = ps.enter_context(nc.psum_tensor("psDm%d" % mt, [128, 4, 512], F32))
                wg_b = ps.enter_context(nc.sbuf_tensor("wg_b%d" % mt, [128, 2, KC, 128], BF16))
                wu_b = ps.enter_context(nc.sbuf_tensor("wu_b%d" % mt, [128, 2, KC, 128], BF16))
                wd_b = ps.enter_context(nc.sbuf_tensor("wd_b%d" % mt, [128, 2, FMC, 512], BF16))
                wtd = ps.enter_context(nc.sbuf_tensor("wtd%d" % mt, [128, FMC, MT], BF16))
                scrm = ps.enter_context(nc.sbuf_tensor("scrm%d" % mt, [128, MT], F32))
                p0 = pe.n
                prev_g = [p0, p0]; prev_u = [p0, p0]; prev_d = [p0, p0]
                wtd_tags = [0] * FMC
                ev_gm = [0]; ev_um = [0]; ev_dm = [0, 0, 0, 0]
                sc_silu = [0]; dv_wtd = [0]; prev_down = [0]; prev_e_ev = [0]
                for e in range(E):
                    for fb in range(FMC):
                        sl = fb % 2
                        wgs, wus = ws[0 + sl], ws[2 + sl]
                        t_wg = planw.load(wgs, wg_b[:, sl], wg_t[e, fb],
                                          war=[(pe, prev_g[sl])])
                        t_wu = planw.load(wus, wu_b[:, sl], wu_t[e, fb],
                                          war=[(pe, prev_u[sl])])

                        @block.tensor
                        def _(t, sl=sl, t_wg=t_wg, wgs=wgs):
                            t.wait_ge(wgs.h, t_wg)
                            if ev_gm[0] > 0:
                                t.wait_ge(sc.h, ev_gm[0])
                            for ts2 in range(TSUB):
                                for k in range(KC):
                                    mm = t.matmul(psGm[:, ts2, :], wg_b[:, sl, k, :],
                                             xn[:, k, ts2 * 512:(ts2 + 1) * 512],
                                             start=(k == 0), stop=(k == KC - 1))
                            mm.then_inc(pe.h, 1)
                        t_gm = pe.inc()
                        prev_g[sl] = t_gm

                        @block.tensor
                        def _(t, sl=sl, t_wu=t_wu, wus=wus):
                            t.wait_ge(wus.h, t_wu)
                            if ev_um[0] > 0:
                                t.wait_ge(dv.h, ev_um[0])
                            for ts2 in range(TSUB):
                                for k in range(KC):
                                    mm = t.matmul(psUm[:, ts2, :], wu_b[:, sl, k, :],
                                             xn[:, k, ts2 * 512:(ts2 + 1) * 512],
                                             start=(k == 0), stop=(k == KC - 1))
                            mm.then_inc(pe.h, 1)
                        t_um = pe.inc()
                        prev_u[sl] = t_um

                        @block.scalar
                        def _(s, sl=sl, t_gm=t_gm):
                            s.wait_ge(pe.h, t_gm)
                            if dv_wtd[0] > 0:
                                s.wait_ge(dv.h, dv_wtd[0])
                            s.activation(scrm[:],
                                         psGm.ap().rearrange("p a b -> p (a b)"),
                                         AF.Silu).then_inc(sc.h, 1)
                        sc_silu[0] = sc.inc()
                        ev_gm[0] = sc_silu[0]

                        @block.vector
                        def _(v, fb=fb, t_um=t_um, t_si=sc_silu[0], pdn=prev_down[0]):
                            v.wait_ge(pe.h, t_um)
                            v.wait_ge(sc.h, t_si)
                            if pdn > 0:
                                v.wait_ge(pe.h, pdn)
                            v.tensor_tensor(wtd[:, fb, :],
                                            psUm.ap().rearrange("p a b -> p (a b)"),
                                            scrm[:], OP.mult).then_inc(dv.h, 1)
                        dv_wtd[0] = dv.inc()
                        ev_um[0] = dv_wtd[0]
                        wtd_tags[fb] = dv_wtd[0]
                    wtd_ready = dv_wtd[0]

                    for hc in range(HC):
                        sl = hc % 2
                        wds = ws[4 + sl]
                        t_wd = planw.load(wds, wd_b[:, sl], wd_t[e, hc],
                                          war=[(pe, prev_d[sl])])
                        for tb in range(TB):
                            psl = [0, 2, 1, 3][(hc * TB + tb) % 4]

                            @block.tensor
                            def _(t, hc=hc, sl=sl, tb=tb, psl=psl, t_wd=t_wd,
                                  wtags=list(wtd_tags)):
                                if tb == 0:
                                    t.wait_ge(ws[4 + hc % 2].h, t_wd)
                                if ev_dm[psl] > 0:
                                    t.wait_ge(dv.h, ev_dm[psl])
                                first = (hc == 0 and tb == 0)
                                for k in range(FMC):
                                    if first:
                                        t.wait_ge(dv.h, wtags[k])
                                    mm = t.matmul(psDm[:, psl, :],
                                             wtd[:, k, tb * 128:(tb + 1) * 128],
                                             wd_b[:, sl, k, :],
                                             start=(k == 0), stop=(k == FMC - 1))
                                mm.then_inc(pe.h, 1)
                            t_dm = pe.inc()

                            @block.vector
                            def _(v, e=e, hc=hc, tb=tb, psl=psl, t_dm=t_dm,
                                  pg=prev_e_ev[0]):
                                v.wait_ge(pe.h, t_dm)
                                if hc == 0 and tb == 0 and pg > 0:
                                    v.wait_ge(dv.h, pg)
                                v.scalar_tensor_tensor(acc[:, tb, hc * 512:(hc + 1) * 512],
                                                       psDm[:, psl, :],
                                                       comb[:, tb, e:e + 1],
                                                       acc[:, tb, hc * 512:(hc + 1) * 512],
                                                       OP.mult, OP.add).then_inc(dv.h, 1)
                            ev_dm[psl] = dv.inc()
                        prev_d[sl] = pe.n
                    prev_down[0] = pe.n
                    prev_e_ev[0] = dv.n
                fence()
            if stop_after == "moe%d" % mt:
                finish()
                return nc

            with ExitStack() as ps:
                psS = ps.enter_context(nc.psum_tensor("psS%d" % mt, [128, 2, 512], F32))
                psU2 = ps.enter_context(nc.psum_tensor("psU2%d" % mt, [128, 2, 512], F32))
                psD2 = ps.enter_context(nc.psum_tensor("psD2%d" % mt, [128, 4, 512], F32))
                sg_bb = ps.enter_context(nc.sbuf_tensor("sg_bb%d" % mt, [128, 2, KC, CBW], BF16))
                su_bb = ps.enter_context(nc.sbuf_tensor("su_bb%d" % mt, [128, 2, KC, CBW], BF16))
                sd_bb = ps.enter_context(nc.sbuf_tensor("sd_bb%d" % mt, [128, 2, 2, D], BF16))
                y_sb = ps.enter_context(nc.sbuf_tensor("y_sb%d" % mt, [128, 2, 2, MT], BF16))
                scrs = ps.enter_context(nc.sbuf_tensor("scrs%d" % mt, [128, MT], F32))
                p0 = pe.n
                prev_sg = [p0, p0]; prev_su = [p0, p0]; prev_sd = [p0, p0]
                prev_y = [p0, p0]
                ev_s = [0]; ev_u2 = [0]; ev_d2 = [0, 0, 0, 0]
                sc_silu2 = [0]; dv_y = [0]; prev_p_ev = [0]
                y_tags = [[0, 0], [0, 0]]
                t_sdw_pair = [0, 0]
                for cb in range(NCB):
                    sl = cb % 2
                    t_sgw = planw.load(ws[0 + sl], sg_bb[:, sl], swg_t[cb],
                                       war=[(pe, prev_sg[sl])])
                    t_suw = planw.load(ws[2 + sl], su_bb[:, sl], swu_t[cb],
                                       war=[(pe, prev_su[sl])])
                    t_sdw_pair[sl] = planw.load(ws[4 + sl], sd_bb[:, sl], swd_t[cb],
                                                war=[(pe, prev_sd[sl])])
                    for ch in range(2):
                        @block.tensor
                        def _(t, sl=sl, ch=ch, t_sgw=t_sgw):
                            if ch == 0:
                                t.wait_ge(ws[0 + sl].h, t_sgw)
                            if ev_s[0] > 0:
                                t.wait_ge(sc.h, ev_s[0])
                            for ts2 in range(TSUB):
                                for k in range(KC):
                                    mm = t.matmul(psS[:, ts2, :],
                                             sg_bb[:, sl, k, ch * 128:(ch + 1) * 128],
                                             xn[:, k, ts2 * 512:(ts2 + 1) * 512],
                                             start=(k == 0), stop=(k == KC - 1))
                            mm.then_inc(pe.h, 1)
                        t_sm = pe.inc()
                        prev_sg[sl] = t_sm

                        @block.tensor
                        def _(t, sl=sl, ch=ch, t_suw=t_suw):
                            if ch == 0:
                                t.wait_ge(ws[2 + sl].h, t_suw)
                            if ev_u2[0] > 0:
                                t.wait_ge(dv.h, ev_u2[0])
                            for ts2 in range(TSUB):
                                for k in range(KC):
                                    mm = t.matmul(psU2[:, ts2, :],
                                             su_bb[:, sl, k, ch * 128:(ch + 1) * 128],
                                             xn[:, k, ts2 * 512:(ts2 + 1) * 512],
                                             start=(k == 0), stop=(k == KC - 1))
                            mm.then_inc(pe.h, 1)
                        t_um2 = pe.inc()
                        prev_su[sl] = t_um2

                        @block.scalar
                        def _(s, t_sm=t_sm):
                            s.wait_ge(pe.h, t_sm)
                            if dv_y[0] > 0:
                                s.wait_ge(dv.h, dv_y[0])
                            s.activation(scrs[:],
                                         psS.ap().rearrange("p a b -> p (a b)"),
                                         AF.Silu).then_inc(sc.h, 1)
                        sc_silu2[0] = sc.inc()
                        ev_s[0] = sc_silu2[0]

                        @block.vector
                        def _(v, sl=sl, ch=ch, t_um2=t_um2, t_si=sc_silu2[0],
                              pv=prev_y[sl]):
                            v.wait_ge(pe.h, t_um2)
                            v.wait_ge(sc.h, t_si)
                            if pv > 0 and ch == 0:
                                v.wait_ge(pe.h, pv)
                            v.tensor_tensor(y_sb[:, sl, ch, :],
                                            psU2.ap().rearrange("p a b -> p (a b)"),
                                            scrs[:], OP.mult).then_inc(dv.h, 1)
                        dv_y[0] = dv.inc()
                        ev_u2[0] = dv_y[0]
                        y_tags[sl][ch] = dv_y[0]

                    if cb % 2 == 1:
                        pair_ready = max(y_tags[0] + y_tags[1])
                        for tb in range(TB):
                            for fc in range(HC):
                                idx = tb * HC + fc
                                psl = idx % 2

                                @block.tensor
                                def _(t, tb=tb, fc=fc, psl=psl, idx=idx,
                                      pair_ready=pair_ready,
                                      t_sd0=t_sdw_pair[0], t_sd1=t_sdw_pair[1]):
                                    if idx == 0:
                                        t.wait_ge(ws[4].h, t_sd0)
                                        t.wait_ge(ws[5].h, t_sd1)
                                        t.wait_ge(dv.h, pair_ready)
                                    if ev_d2[psl] > 0:
                                        t.wait_ge(dv.h, ev_d2[psl])
                                    for k in range(4):
                                        slk, chk = (0, k) if k < 2 else (1, k - 2)
                                        mm = t.matmul(psD2[:, psl, :],
                                                 y_sb[:, slk, chk, tb * 128:(tb + 1) * 128],
                                                 sd_bb[:, slk, chk, fc * 512:(fc + 1) * 512],
                                                 start=(k == 0), stop=(k == 3))
                                    mm.then_inc(pe.h, 1)
                                t_d2 = pe.inc()

                                @block.vector
                                def _(v, tb=tb, fc=fc, psl=psl, t_d2=t_d2,
                                      idx=idx, pg=prev_p_ev[0]):
                                    v.wait_ge(pe.h, t_d2)
                                    if idx == 0 and pg > 0:
                                        v.wait_ge(dv.h, pg)
                                    v.scalar_tensor_tensor(acc[:, tb, fc * 512:(fc + 1) * 512],
                                                           psD2[:, psl, :],
                                                           gt[:, tb, 0:1],
                                                           acc[:, tb, fc * 512:(fc + 1) * 512],
                                                           OP.mult, OP.add).then_inc(dv.h, 1)
                                ev_d2[psl] = dv.inc()
                        prev_p_ev[0] = dv.n
                        prev_y = [pe.n, pe.n]
                        prev_sd = [pe.n, pe.n]
                fence()
            if stop_after == "shared%d" % mt:
                finish()
                return nc

            last_ev = dv.n
            t_st = planx.load(
                qs, out[tok0:tok0 + MT, :].rearrange("(a p) d -> p a d", p=128),
                acc[:], war=[(dv, last_ev), (qs, qs.n - 16)])
            store_tags.append(t_st)

        finish()

    return nc



def _tile_lhsT_cols(wT, nblk, blkw):
    K = wT.shape[0]
    kc = K // 128
    a = wT.reshape(kc, 128, nblk, blkw)
    return np.ascontiguousarray(a.transpose(2, 1, 0, 3))


def _tile_rhs_rows(w, nblk, kper, nw):
    a = w.reshape(nblk, kper, 128, nw)
    return np.ascontiguousarray(a.transpose(0, 2, 1, 3))


_NC_CACHE = {}


def kernel(L0_attn_x, L0_attn_Wq, L0_attn_Wk, L0_attn_Wv, L0_attn_Wo,
           L0_ffn_router_weight, L0_ffn_w_gate, L0_ffn_w_up, L0_ffn_w_down,
           L0_ffn_sw_gate, L0_ffn_sw_up, L0_ffn_sw_down, L0_ffn_sgate_w):
    x = np.ascontiguousarray(np.asarray(L0_attn_x, dtype=np.float32))
    Wq = np.asarray(L0_attn_Wq, dtype=np.float32)
    Wk = np.asarray(L0_attn_Wk, dtype=np.float32)
    Wv = np.asarray(L0_attn_Wv, dtype=np.float32)
    Wo = np.asarray(L0_attn_Wo, dtype=np.float32)
    rw = np.asarray(L0_ffn_router_weight, dtype=np.float32)
    wg = np.asarray(L0_ffn_w_gate, dtype=np.float32)
    wu = np.asarray(L0_ffn_w_up, dtype=np.float32)
    wd = np.asarray(L0_ffn_w_down, dtype=np.float32)
    swg = np.asarray(L0_ffn_sw_gate, dtype=np.float32)
    swu = np.asarray(L0_ffn_sw_up, dtype=np.float32)
    swd = np.asarray(L0_ffn_sw_down, dtype=np.float32)
    sgw = np.asarray(L0_ffn_sgate_w, dtype=np.float32)

    feeds = dict(
        wq_t=_tile_lhsT_cols(np.ascontiguousarray(Wq.T), KC, 128),
        wk_t=_tile_lhsT_cols(np.ascontiguousarray(Wk.T), KC, 128),
        wv_t=_tile_lhsT_cols(np.ascontiguousarray(Wv.T), KC, 128),
        wo_t=_tile_rhs_rows(np.ascontiguousarray(Wo.T), 4, 4, D),
        rw_t=_tile_lhsT_cols(np.ascontiguousarray(rw.T), 1, E)[0],
        sgw_t=_tile_lhsT_cols(np.ascontiguousarray(sgw), 1, 1)[0],
        wg_t=np.stack([_tile_lhsT_cols(wg[e], FMC, 128) for e in range(E)]),
        wu_t=np.stack([_tile_lhsT_cols(wu[e], FMC, 128) for e in range(E)]),
        wd_t=np.stack([np.ascontiguousarray(
            wd[e].reshape(FMC, 128, HC, 512).transpose(2, 1, 0, 3)) for e in range(E)]),
        swg_t=_tile_lhsT_cols(swg, NCB, CBW),
        swu_t=_tile_lhsT_cols(swu, NCB, CBW),
        swd_t=_tile_rhs_rows(swd, NCB, 2, D),
        ident=np.eye(128, dtype=ml_dtypes.bfloat16),
        identf=np.eye(128, dtype=np.float32),
    )

    if "nc" not in _NC_CACHE:
        _NC_CACHE["nc"] = build_nc()
    nc = _NC_CACHE["nc"]

    in_maps = []
    for c in range(8):
        b, h = c // 2, c % 2
        m = dict(feeds)
        m["x_own"] = np.ascontiguousarray(x[b, h * TCORE:(h + 1) * TCORE, :])
        m["x_pre"] = np.ascontiguousarray(x[b, 0:TCORE, :])
        m["flag"] = np.full((128, 1), float(h), dtype=np.float32)
        in_maps.append(m)

    res = run_bass_kernel_spmd(nc, in_maps, core_ids=list(range(8)))
    full = np.empty((4, 4096, D), dtype=np.float32)
    for c in range(8):
        b, h = c // 2, c % 2
        full[b, h * TCORE:(h + 1) * TCORE, :] = np.asarray(res.results[c]["out"])
    return full


# version 10
# speedup vs baseline: 1.0847x; 1.0847x over previous
import numpy as np
import ml_dtypes
from contextlib import ExitStack

import concourse.bass as bass
import concourse.mybir as mybir
from concourse.bass_utils import run_bass_kernel_spmd

F32 = mybir.dt.float32
BF16 = mybir.dt.bfloat16
AF = mybir.ActivationFunctionType
OP = mybir.AluOpType
AX = mybir.AxisListType

D = 2048
KC = 16
TCORE = 2048
MT = 1024
NMT = TCORE // MT
TSUB = MT // 512
TB = MT // 128
E = 32
FMC = 6
HC = 4
FS = 4096
CBW = 256
NCB = FS // CBW
DENOM = float(np.sqrt(np.float32(2048.0)) / 4096.0)
EPS = 1e-5


class S:
    def __init__(self, h):
        self.h = h
        self.n = 0

    def inc(self, k=1):
        self.n += k
        return self.n


class Plan:
    def __init__(self):
        self.items = []

    def load(self, sem, out_ap, in_ap, war=None):
        self.items.append((sem, out_ap, in_ap, list(war or [])))
        return sem.inc(16)


def build_nc(stop_after=None, debug=False, pref_ebs=KC):
    nc = bass.Bass("TRN2", target_bir_lowering=False)

    x_own = nc.dram_tensor("x_own", [TCORE, D], F32, kind="ExternalInput").ap()
    x_pre = nc.dram_tensor("x_pre", [TCORE, D], F32, kind="ExternalInput").ap()
    flag = nc.dram_tensor("flag", [128, 1], F32, kind="ExternalInput").ap()
    ident = nc.dram_tensor("ident", [128, 128], BF16, kind="ExternalInput").ap()
    identf = nc.dram_tensor("identf", [128, 128], F32, kind="ExternalInput").ap()
    wq_t = nc.dram_tensor("wq_t", [KC, 128, KC, 128], F32, kind="ExternalInput").ap()
    wk_t = nc.dram_tensor("wk_t", [KC, 128, KC, 128], F32, kind="ExternalInput").ap()
    wv_t = nc.dram_tensor("wv_t", [KC, 128, KC, 128], F32, kind="ExternalInput").ap()
    wo_t = nc.dram_tensor("wo_t", [4, 128, 4, D], F32, kind="ExternalInput").ap()
    rw_t = nc.dram_tensor("rw_t", [128, KC, E], F32, kind="ExternalInput").ap()
    sgw_t = nc.dram_tensor("sgw_t", [128, KC, 1], F32, kind="ExternalInput").ap()
    wg_t = nc.dram_tensor("wg_t", [E, FMC, 128, KC, 128], F32, kind="ExternalInput").ap()
    wu_t = nc.dram_tensor("wu_t", [E, FMC, 128, KC, 128], F32, kind="ExternalInput").ap()
    wd_t = nc.dram_tensor("wd_t", [E, HC, 128, FMC, 512], F32, kind="ExternalInput").ap()
    swg_t = nc.dram_tensor("swg_t", [NCB, 128, KC, CBW], F32, kind="ExternalInput").ap()
    swu_t = nc.dram_tensor("swu_t", [NCB, 128, KC, CBW], F32, kind="ExternalInput").ap()
    swd_t = nc.dram_tensor("swd_t", [NCB, 128, 2, D], F32, kind="ExternalInput").ap()
    out = nc.dram_tensor("out", [TCORE, D], F32, kind="ExternalOutput").ap()
    if debug:
        d_xn = nc.dram_tensor("d_xn", [128, KC, MT], BF16, kind="ExternalOutput").ap()
        d_carry = nc.dram_tensor("d_carry", [128, KC], F32, kind="ExternalOutput").ap()
        d_pref = nc.dram_tensor("d_pref", [128, KC], F32, kind="ExternalOutput").ap()
        d_comb = nc.dram_tensor("d_comb", [128, TB, E], F32, kind="ExternalOutput").ap()
        d_gt = nc.dram_tensor("d_gt", [128, TB], F32, kind="ExternalOutput").ap()
        d_acc = nc.dram_tensor("d_acc", [128, TB, D], F32, kind="ExternalOutput").ap()

    with ExitStack() as es:
        acc = es.enter_context(nc.sbuf_tensor("acc", [128, TB, D], F32))
        xn = es.enter_context(nc.sbuf_tensor("xn", [128, KC, MT], BF16))
        ones = es.enter_context(nc.sbuf_tensor("ones", [128, MT], F32))
        carry = es.enter_context(nc.sbuf_tensor("carry", [128, KC], F32))
        pref = es.enter_context(nc.sbuf_tensor("pref", [128, KC], F32))
        flag_sb = es.enter_context(nc.sbuf_tensor("flag_sb", [128, 1], F32))
        id_sb = es.enter_context(nc.sbuf_tensor("id_sb", [128, 128], BF16))
        idf_sb = es.enter_context(nc.sbuf_tensor("idf_sb", [128, 128], F32))
        ones1 = es.enter_context(nc.sbuf_tensor("ones1", [1, 128], F32))
        comb_fm = es.enter_context(nc.sbuf_tensor("comb_fm", [32, MT], F32))
        comb = es.enter_context(nc.sbuf_tensor("comb", [128, TB, E], F32))
        gt = es.enter_context(nc.sbuf_tensor("gt", [128, TB, 1], F32))
        cur2 = {}
        cur = {}
        lg = es.enter_context(nc.sbuf_tensor("lg", [128, E], F32))
        eee = es.enter_context(nc.sbuf_tensor("eee", [128, E], F32))
        msk = es.enter_context(nc.sbuf_tensor("msk", [128, E], F32))
        top8 = es.enter_context(nc.sbuf_tensor("top8", [128, 8], F32))
        s_a = es.enter_context(nc.sbuf_tensor("s_a", [128, 1], F32))
        s_b = es.enter_context(nc.sbuf_tensor("s_b", [128, 1], F32))
        s_c = es.enter_context(nc.sbuf_tensor("s_c", [128, 1], F32))
        s_d = es.enter_context(nc.sbuf_tensor("s_d", [128, 1], F32))
        s_e = es.enter_context(nc.sbuf_tensor("s_e", [128, 1], F32))
        eps_sb = es.enter_context(nc.sbuf_tensor("eps_sb", [128, 1], F32))

        pe = S(es.enter_context(nc.semaphore("pe")))
        dv = S(es.enter_context(nc.semaphore("dv")))
        sc = S(es.enter_context(nc.semaphore("sc")))
        ws = [S(es.enter_context(nc.semaphore("w%d" % i))) for i in range(6)]
        qc = S(es.enter_context(nc.semaphore("qc")))
        xa = S(es.enter_context(nc.semaphore("xa")))
        xb = S(es.enter_context(nc.semaphore("xb")))
        xc = S(es.enter_context(nc.semaphore("xc")))
        qs = S(es.enter_context(nc.semaphore("qs")))
        dma_sems = ws + [qc, xa, xb, xc, qs]

        block = es.enter_context(nc.Block())
        planw = Plan()
        planx = Plan()

        def emit_plan(q, plan):
            for dsem, out_ap, in_ap, war in plan.items:
                for (sem, val) in war:
                    if val > 0:
                        q.wait_ge(sem.h, val)
                q.dma_start(out=out_ap, in_=in_ap).then_inc(dsem.h, 16)
            plan.items = []

        def vsync(v):
            v.drain()
            v.sem_inc(dv.h, 1)
            dv.inc()
            v.wait_ge(dv.h, dv.n)

        def ssync(s):
            s.drain()
            s.sem_inc(sc.h, 1)
            sc.inc()
            s.wait_ge(sc.h, sc.n)

        def barrier():
            @block.gpsimd
            def _(g):
                emit_plan(g, planw)

            @block.sync
            def _(sy):
                emit_plan(sy, planx)

            @block.tensor
            def _(t):
                t.drain()
                t.sem_inc(pe.h, 1)
            pe.inc()

            @block.vector
            def _(v):
                v.drain()
                v.sem_inc(dv.h, 1)
            dv.inc()

            @block.scalar
            def _(s):
                s.drain()
                s.sem_inc(sc.h, 1)
            sc.inc()
            targets = ([(pe, pe.n), (dv, dv.n), (sc, sc.n)]
                       + [(s2, s2.n) for s2 in dma_sems])

            @block.tensor
            def _(t):
                for sem, val in targets:
                    if sem is not pe:
                        t.wait_ge(sem.h, val)

            @block.vector
            def _(v):
                for sem, val in targets:
                    if sem is not dv:
                        v.wait_ge(sem.h, val)

            @block.scalar
            def _(s):
                for sem, val in targets:
                    if sem is not sc:
                        s.wait_ge(sem.h, val)

            @block.gpsimd
            def _(g):
                for sem, val in targets:
                    g.wait_ge(sem.h, val)

            @block.sync
            def _(sy):
                for sem, val in targets:
                    sy.wait_ge(sem.h, val)

        def finish():
            @block.gpsimd
            def _(g):
                emit_plan(g, planw)

            @block.sync
            def _(sy):
                emit_plan(sy, planx)
                if debug:
                    sy.wait_ge(dv.h, dv.n)
                    sy.wait_ge(sc.h, sc.n)
                    sy.dma_start(out=d_xn, in_=xn[:]).then_inc(qs.h, 16)
                    sy.dma_start(out=d_carry, in_=carry[:]).then_inc(qs.h, 16)
                    sy.dma_start(out=d_pref, in_=pref[:]).then_inc(qs.h, 16)
                    sy.dma_start(out=d_comb, in_=comb[:]).then_inc(qs.h, 16)
                    sy.dma_start(out=d_gt, in_=gt.ap().rearrange("p a b -> p (a b)")).then_inc(qs.h, 16)
                    sy.dma_start(out=d_acc, in_=acc[:]).then_inc(qs.h, 16)
                    qs.inc(16 * 6)
                for s2 in dma_sems:
                    sy.wait_ge(s2.h, s2.n)

            @block.vector
            def _(v):
                for s2 in dma_sems:
                    v.wait_ge(s2.h, s2.n)

        def fence():
            @block.gpsimd
            def _(g):
                emit_plan(g, planw)

            @block.sync
            def _(sy):
                emit_plan(sy, planx)
            pn, dn, sn = pe.n, dv.n, sc.n

            @block.tensor
            def _(t):
                t.wait_ge(dv.h, dn)
                t.wait_ge(sc.h, sn)

            @block.vector
            def _(v):
                v.wait_ge(pe.h, pn)
                v.wait_ge(sc.h, sn)

            @block.scalar
            def _(s):
                s.wait_ge(pe.h, pn)
                s.wait_ge(dv.h, dn)

        planx.load(qc, id_sb[:], ident)
        planx.load(qc, idf_sb[:], identf)
        t_consts = planx.load(qc, flag_sb[:], flag)

        @block.vector
        def _(v):
            v.memset(ones[:], 1.0)
            v.memset(pref[:], 0.0)
            v.memset(carry[:], 0.0)
            v.memset(eps_sb[:], EPS)
            v.memset(ones1[:], 1.0)
            if debug:
                v.memset(acc[:], 0.0)
                v.memset(comb[:], 0.0)
                v.memset(gt[:], 0.0)
                v.memset(xn[:], 0.0)
            v.drain()
            v.sem_inc(dv.h, 1)
        dv.inc()

        if stop_after == "init":
            finish()
            return nc

        def ln_emit(chunks, psT, dst_fm, router=None):
            n = len(chunks)
            xn_tag = [0] * n; sq_tag = [0] * n; var_tag = [0] * n
            trlast = [0] * n; cplast = [0] * n
            cp_slot = [0, 0]
            rmm_tag = [0] * n; gmm_tag = [0] * n; lgcp = [0] * n; sig = [0] * n
            comb_tag = [0] * n; cfcp = [0] * n

            def chain(t):
                ch = chunks[t]
                src = ch["src"]
                if "load" in ch:
                    war = []
                    if t >= 3:
                        war = [(dv, xn_tag[t - 3]), (sc, sq_tag[t - 3])]
                    ready = ch["load"](war)
                else:
                    ready = ch["ready"]
                par = t % 3
                s2a, s2b, s2c, s2d, xn_tm2 = (cur2["s2a"], cur2["s2b"],
                                              cur2["s2c"], cur2["s2d"],
                                              cur2["xn3"])
                sa = s2a[:, par:par + 1]; sb = s2b[:, par:par + 1]
                sc_ = s2c[:, par:par + 1]; sd = s2d[:, par:par + 1]

                @block.vector
                def _(v):
                    for sem, val in ready:
                        if val > 0:
                            v.wait_ge(sem.h, val)
                    if t >= 3 and trlast[t - 3] > 0:
                        v.wait_ge(pe.h, trlast[t - 3])
                    v.tensor_reduce(sa, src, AX.X, OP.add)
                    vsync(v)

                @block.scalar
                def _(s):
                    for sem, val in ready:
                        if val > 0:
                            s.wait_ge(sem.h, val)
                    if t >= 3:
                        s.wait_ge(dv.h, var_tag[t - 3])
                    s.activation(cur["scr"][:], src, AF.Square,
                                 accum_out=sb).then_inc(sc.h, 1)
                sq_tag[t] = sc.inc()

                @block.vector
                def _(v):
                    v.wait_ge(sc.h, sq_tag[t])
                    v.tensor_scalar(sa, sa, 1.0 / D, None, OP.mult)
                    v.tensor_scalar(sb, sb, 1.0 / D, None, OP.mult)
                    vsync(v)
                    v.tensor_tensor(sc_, sa, sa, OP.mult)
                    vsync(v)
                    v.tensor_tensor(sc_, sb, sc_, OP.subtract)
                    v.drain()
                    v.sem_inc(dv.h, 1)
                var_tag[t] = dv.inc()

                @block.scalar
                def _(s):
                    s.wait_ge(dv.h, var_tag[t])
                    s.activation(sd, sc_, AF.Ln, bias=eps_sb[:, 0:1])
                    ssync(s)
                    s.activation(sd, sd, AF.Exp, scale=-0.5)
                    s.drain()
                    s.sem_inc(sc.h, 1)
                t_sd = sc.inc()

                @block.vector
                def _(v):
                    v.wait_ge(sc.h, t_sd)
                    if t >= 2 and trlast[t - 2] == 0:
                        pass
                    v.tensor_scalar(xn_tm2[:, par, :], src, sa, sd,
                                    OP.subtract, OP.mult)
                    v.drain()
                    v.sem_inc(dv.h, 1)
                xn_tag[t] = dv.inc()

            def trcopies(t):
                col0 = chunks[t]["col0"]
                par = t % 3
                xn_tm2 = cur2["xn3"]
                for fb in range(KC):
                    @block.tensor
                    def _(t_, fb=fb):
                        if fb == 0:
                            t_.wait_ge(dv.h, xn_tag[t])
                            t_.wait_ge(qc.h, t_consts)
                        if cp_slot[fb % 2] > 0:
                            t_.wait_ge(dv.h, cp_slot[fb % 2])
                        t_.transpose(psT[:, fb % 2, 0:128],
                                     xn_tm2[:, par, fb * 128:(fb + 1) * 128],
                                     id_sb[:]).then_inc(pe.h, 1)
                    t_tr = pe.inc()

                    @block.vector
                    def _(v, fb=fb, t_tr=t_tr):
                        v.wait_ge(pe.h, t_tr)
                        v.tensor_copy(dst_fm[:, fb, col0:col0 + 128],
                                      psT[:, fb % 2, 0:128]).then_inc(dv.h, 1)
                    cp_slot[fb % 2] = dv.inc()
                trlast[t] = pe.n
                cplast[t] = dv.n

            def router_pe(t):
                r = router
                par = t % 2

                @block.tensor
                def _(t_):
                    if t == 0:
                        t_.wait_ge(ws[0].h, r["t_rw"])
                        t_.wait_ge(ws[2].h, r["t_sg"])
                    t_.wait_ge(dv.h, cplast[t])
                    if t >= 2:
                        t_.wait_ge(dv.h, lgcp[t - 2])
                    mm = None
                    for k in range(KC):
                        mm = t_.matmul(r["psR"][:, par, 0:E],
                                       xn[:, k, t * 128:(t + 1) * 128],
                                       r["rw_b"][:, k, :],
                                       start=(k == 0), stop=(k == KC - 1))
                    mm.then_inc(pe.h, 1)
                rmm_tag[t] = pe.inc()

                @block.tensor
                def _(t_):
                    if t >= 2:
                        t_.wait_ge(sc.h, sig[t - 2])
                    mm = None
                    for k in range(KC):
                        mm = t_.matmul(r["psGt"][:, par, 0:1],
                                       xn[:, k, t * 128:(t + 1) * 128],
                                       r["sg_b"][:, k, :],
                                       start=(k == 0), stop=(k == KC - 1))
                    mm.then_inc(pe.h, 1)
                gmm_tag[t] = pe.inc()

            def router_rest(t):
                r = router
                par = t % 2

                @block.scalar
                def _(s):
                    s.wait_ge(pe.h, gmm_tag[t])
                    if t >= 1:
                        s.wait_ge(dv.h, sig_dv[t - 1])
                    s.activation(s_e[:], r["psGt"][:, par, 0:1], AF.Exp,
                                 scale=-1.0).then_inc(sc.h, 1)
                sig[t] = sc.inc()

                @block.vector
                def _(v):
                    v.wait_ge(sc.h, sig[t])
                    v.tensor_scalar(gt[:, t, :], s_e[:], 1.0, None, OP.add)
                    vsync(v)
                    v.reciprocal(gt[:, t, :], gt[:, t, :])
                    v.drain()
                    v.sem_inc(dv.h, 1)
                sig_dv[t] = dv.inc()

                @block.vector
                def _(v):
                    v.wait_ge(pe.h, rmm_tag[t])
                    v.tensor_copy(lg[:], r["psR"][:, par, 0:E]).then_inc(dv.h, 1)
                    vsync(v)
                    v.tensor_reduce(s_a[:], lg[:], AX.X, OP.max)
                    vsync(v)
                    v.tensor_scalar(s_a[:], s_a[:], -1.0, None, OP.mult)
                    v.drain()
                    v.sem_inc(dv.h, 1)
                lgcp[t] = dv.inc()
                t_nm = dv.inc()

                @block.scalar
                def _(s):
                    s.wait_ge(dv.h, t_nm)
                    s.activation(eee[:], lg[:], AF.Exp, bias=s_a[:],
                                 scale=1.0).then_inc(sc.h, 1)
                t_ex = sc.inc()

                @block.vector
                def _(v):
                    v.wait_ge(sc.h, t_ex)
                    v.max(top8[:], eee[:])
                    vsync(v)
                    v.tensor_reduce(s_b[:], top8[:, 0:4], AX.X, OP.add)
                    vsync(v)
                    v.reciprocal(s_b[:], s_b[:])
                    vsync(v)
                    v.tensor_scalar(msk[:], eee[:], top8[:, 3:4], None, OP.is_ge)
                    vsync(v)
                    v.tensor_tensor(msk[:], eee[:], msk[:], OP.mult)
                    vsync(v)
                    v.tensor_scalar(comb[:, t, :], msk[:], s_b[:, 0:1], None, OP.mult)
                    v.drain()
                    v.sem_inc(dv.h, 1)
                comb_tag[t] = dv.inc()

            def cfT(t):
                r = router

                @block.tensor
                def _(t_):
                    t_.wait_ge(dv.h, comb_tag[t])
                    if t >= 2:
                        t_.wait_ge(dv.h, cfcp[t - 2])
                    t_.transpose(r["psCF"][0:E, t % 2, 0:128], comb[:, t, :],
                                 idf_sb[:]).then_inc(pe.h, 1)
                t_cf = pe.inc()

                @block.vector
                def _(v):
                    v.wait_ge(pe.h, t_cf)
                    v.tensor_copy(comb_fm[:, t * 128:(t + 1) * 128],
                                  r["psCF"][0:E, t % 2, 0:128]).then_inc(dv.h, 1)
                cfcp[t] = dv.inc()

            sig_dv = [0] * n
            for step in range(n + 5):
                if 2 <= step and step - 2 < n:
                    trcopies(step - 2)
                if step < n:
                    chain(step)
                if router is not None and 3 <= step and step - 3 < n:
                    router_pe(step - 3)
                    router_rest(step - 3)
                if router is not None and 4 <= step and step - 4 < n:
                    cfT(step - 4)
            return dv.n

        for half in range(2):
            with ExitStack() as ps:
                psT = ps.enter_context(nc.psum_tensor("psTp%d" % half, [128, 2, 2048], BF16))
                psK = ps.enter_context(nc.psum_tensor("psKp%d" % half, [128, 2, 512], F32))
                psV = ps.enter_context(nc.psum_tensor("psVp%d" % half, [128, 2, 512], F32))
                xld = ps.enter_context(nc.sbuf_tensor("xldp%d" % half, [128, 3, D], F32))
                cur["scr"] = ps.enter_context(nc.sbuf_tensor("scrp%d" % half, [128, D], F32))
                kscr = ps.enter_context(nc.sbuf_tensor("kscrp%d" % half, [128, MT], F32))
                kvscr = ps.enter_context(nc.sbuf_tensor("kvscrp%d" % half, [128, MT], F32))
                wk_b = ps.enter_context(nc.sbuf_tensor("wkp%d" % half, [128, 2, KC, 128], BF16))
                wv_b = ps.enter_context(nc.sbuf_tensor("wvp%d" % half, [128, 2, KC, 128], BF16))

                base_dv, base_sc = dv.n, sc.n
                cur2["xn3"] = ps.enter_context(nc.sbuf_tensor("xn3p%d" % half, [128, 3, D], BF16))
                cur2["s2a"] = ps.enter_context(nc.sbuf_tensor("s2ap%d" % half, [128, 3], F32))
                cur2["s2b"] = ps.enter_context(nc.sbuf_tensor("s2bp%d" % half, [128, 3], F32))
                cur2["s2c"] = ps.enter_context(nc.sbuf_tensor("s2cp%d" % half, [128, 3], F32))
                cur2["s2d"] = ps.enter_context(nc.sbuf_tensor("s2dp%d" % half, [128, 3], F32))

                def mkload(ch):
                    sl = ch % 3
                    xsem = [xa, xb, xc][sl]
                    r0 = half * MT + ch * 128

                    def load(war):
                        w2 = [(dv, base_dv), (sc, base_sc)] if not war else war
                        t_x = planx.load(xsem, xld[:, sl, :],
                                         x_pre[r0:r0 + 128, :], war=w2)
                        return [(xsem, t_x)]
                    return load
                chunks = [dict(src=xld[:, ch % 3, :], col0=ch * 128,
                               load=mkload(ch)) for ch in range(TB)]
                ln_emit(chunks, psT, xn)
                xn_ready = dv.n
                if stop_after == "prefln%d" % half:
                    barrier()
                    finish()
                    return nc

                p0 = pe.n
                prev_use = {"k0": p0, "k1": p0, "v0": p0, "v1": p0}
                evK, evV = [0], [0]
                for eb in range(pref_ebs):
                    sl = eb % 2
                    wks = ws[0 + sl]
                    wvs = ws[2 + sl]
                    t_wk = planw.load(wks, wk_b[:, sl], wk_t[eb],
                                      war=[(pe, prev_use["k%d" % sl])])
                    t_wv = planw.load(wvs, wv_b[:, sl], wv_t[eb],
                                      war=[(pe, prev_use["v%d" % sl])])

                    @block.tensor
                    def _(t, sl=sl, t_wk=t_wk, eb=eb, wks=wks):
                        t.wait_ge(wks.h, t_wk)
                        if eb == 0:
                            t.wait_ge(dv.h, xn_ready)
                        if evK[0] > 0:
                            t.wait_ge(dv.h, evK[0])
                        for ts2 in range(TSUB):
                            for k in range(KC):
                                mm = t.matmul(psK[:, ts2, :], wk_b[:, sl, k, :],
                                         xn[:, k, ts2 * 512:(ts2 + 1) * 512],
                                         start=(k == 0), stop=(k == KC - 1))
                        mm.then_inc(pe.h, 1)
                    t_kmm = pe.inc()
                    prev_use["k%d" % sl] = t_kmm

                    @block.tensor
                    def _(t, sl=sl, t_wv=t_wv, wvs=wvs):
                        t.wait_ge(wvs.h, t_wv)
                        if evV[0] > 0:
                            t.wait_ge(dv.h, evV[0])
                        for ts2 in range(TSUB):
                            for k in range(KC):
                                mm = t.matmul(psV[:, ts2, :], wv_b[:, sl, k, :],
                                         xn[:, k, ts2 * 512:(ts2 + 1) * 512],
                                         start=(k == 0), stop=(k == KC - 1))
                        mm.then_inc(pe.h, 1)
                    t_vmm = pe.inc()
                    prev_use["v%d" % sl] = t_vmm

                    @block.vector
                    def _(v, t_kmm=t_kmm, t_vmm=t_vmm, eb=eb):
                        v.wait_ge(pe.h, t_kmm)
                        v.tensor_copy(kscr[:], psK.ap().rearrange("p a b -> p (a b)")).then_inc(dv.h, 1)
                    evK[0] = dv.inc()

                    @block.vector
                    def _(v, t_vmm=t_vmm, eb=eb, t_k=evK[0]):
                        v.wait_ge(pe.h, t_vmm)
                        v.wait_ge(dv.h, t_k)
                        v.tensor_tensor(kvscr[:], psV.ap().rearrange("p a b -> p (a b)"),
                                        kscr[:], OP.mult).then_inc(dv.h, 1)
                    evV[0] = dv.inc()

                    @block.vector
                    def _(v, eb=eb):
                        vsync(v)
                        v.tensor_reduce(s_e[:], kvscr[:], AX.X, OP.add)
                        vsync(v)
                        v.tensor_tensor(pref[:, eb:eb + 1], pref[:, eb:eb + 1],
                                        s_e[:], OP.add)
                barrier()
            if stop_after == "prefkv%d" % half:
                finish()
                return nc

        @block.vector
        def _(v):
            v.wait_ge(qc.h, t_consts)
            v.tensor_scalar(carry[:], pref[:], flag_sb[:, 0:1], None, OP.mult)
            v.drain()
            v.sem_inc(dv.h, 1)
        dv.inc()

        if stop_after == "prefix":
            finish()
            return nc

        store_tags = []
        for mt in range(NMT):
            tok0 = mt * MT

            with ExitStack() as ps:
                psT = ps.enter_context(nc.psum_tensor("psT%d" % mt, [128, 2, 1024], BF16))
                psR = ps.enter_context(nc.psum_tensor("psR%d" % mt, [128, 2, 512], F32))
                psCF = ps.enter_context(nc.psum_tensor("psCF%d" % mt, [128, 2, 512], F32))
                psG = ps.enter_context(nc.psum_tensor("psGt%d" % mt, [128, 2, 512], F32))
                cur["scr"] = ps.enter_context(nc.sbuf_tensor("scrl%d" % mt, [128, D], F32))
                rw_b = ps.enter_context(nc.sbuf_tensor("rw_b%d" % mt, [128, KC, E], BF16))
                sg_b = ps.enter_context(nc.sbuf_tensor("sg_b%d" % mt, [128, KC, 1], BF16))
                p0 = pe.n
                t_rw = planw.load(ws[0], rw_b[:], rw_t, war=[(pe, p0)])
                t_sg = planw.load(ws[2], sg_b[:], sgw_t, war=[(pe, p0)])
                cur2["xn3"] = ps.enter_context(nc.sbuf_tensor("xn3l%d" % mt, [128, 3, D], BF16))
                cur2["s2a"] = ps.enter_context(nc.sbuf_tensor("s2al%d" % mt, [128, 3], F32))
                cur2["s2b"] = ps.enter_context(nc.sbuf_tensor("s2bl%d" % mt, [128, 3], F32))
                cur2["s2c"] = ps.enter_context(nc.sbuf_tensor("s2cl%d" % mt, [128, 3], F32))
                cur2["s2d"] = ps.enter_context(nc.sbuf_tensor("s2dl%d" % mt, [128, 3], F32))
                st_war = [(qs, store_tags[-1])] if store_tags else []

                def mkload(tb):
                    xsem = [xa, xb, xc][tb % 3]
                    r0 = tok0 + tb * 128

                    def load(war):
                        t_x = planx.load(xsem, acc[:, tb, :],
                                         x_own[r0:r0 + 128, :],
                                         war=(war or []) + st_war)
                        return [(xsem, t_x)]
                    return load
                router_ctx = dict(psR=psR, psGt=psG, psCF=psCF, rw_b=rw_b,
                                  sg_b=sg_b, t_rw=t_rw, t_sg=t_sg)
                chunks = [dict(src=acc[:, tb, :], col0=tb * 128,
                               load=mkload(tb)) for tb in range(TB)]
                ln_emit(chunks, psT, xn, router=router_ctx)
                fence()
            if stop_after == "ln%d" % mt:
                finish()
                return nc
            if stop_after == "router%d" % mt:
                finish()
                return nc

            with ExitStack() as ps:
                vhat = ps.enter_context(nc.sbuf_tensor("vhat%d" % mt, [128, KC, MT], BF16))
                vhat_tags = []
                with ExitStack() as ps2:
                    psK = ps2.enter_context(nc.psum_tensor("psK%d" % mt, [128, 2, 512], F32))
                    psV = ps2.enter_context(nc.psum_tensor("psV%d" % mt, [128, 2, 512], F32))
                    psQ = ps2.enter_context(nc.psum_tensor("psQ%d" % mt, [128, 2, 512], F32))
                    wq_b = ps2.enter_context(nc.sbuf_tensor("wqa%d" % mt, [128, 2, KC, 128], BF16))
                    wk_b = ps2.enter_context(nc.sbuf_tensor("wka%d" % mt, [128, 2, KC, 128], BF16))
                    wv_b = ps2.enter_context(nc.sbuf_tensor("wva%d" % mt, [128, 2, KC, 128], BF16))
                    kscr = ps2.enter_context(nc.sbuf_tensor("kscr%d" % mt, [128, MT], F32))
                    kvscr = ps2.enter_context(nc.sbuf_tensor("kvscr%d" % mt, [128, MT], F32))
                    kvtscr = ps2.enter_context(nc.sbuf_tensor("kvtscr%d" % mt, [128, MT], F32))
                    p0 = pe.n
                    prev_use = {"q0": p0, "q1": p0, "k0": p0, "k1": p0,
                                "v0": p0, "v1": p0}
                    evK, evV, evQ = [0], [0], [0]
                    for eb in range(KC):
                        sl = eb % 2
                        wqs, wks, wvs = ws[0 + sl], ws[2 + sl], ws[4 + sl]
                        t_wq = planw.load(wqs, wq_b[:, sl], wq_t[eb],
                                          war=[(pe, prev_use.get("q%d" % sl, 0))])
                        t_wk = planw.load(wks, wk_b[:, sl], wk_t[eb],
                                          war=[(pe, prev_use.get("k%d" % sl, 0))])
                        t_wv = planw.load(wvs, wv_b[:, sl], wv_t[eb],
                                          war=[(pe, prev_use.get("v%d" % sl, 0))])

                        @block.tensor
                        def _(t, sl=sl, t_wk=t_wk, wks=wks):
                            t.wait_ge(wks.h, t_wk)
                            if evK[0] > 0:
                                t.wait_ge(dv.h, evK[0])
                            for ts2 in range(TSUB):
                                for k in range(KC):
                                    mm = t.matmul(psK[:, ts2, :], wk_b[:, sl, k, :],
                                             xn[:, k, ts2 * 512:(ts2 + 1) * 512],
                                             start=(k == 0), stop=(k == KC - 1))
                            mm.then_inc(pe.h, 1)
                        t_kmm = pe.inc()
                        prev_use["k%d" % sl] = t_kmm

                        @block.tensor
                        def _(t, sl=sl, t_wv=t_wv, wvs=wvs):
                            t.wait_ge(wvs.h, t_wv)
                            if evV[0] > 0:
                                t.wait_ge(dv.h, evV[0])
                            for ts2 in range(TSUB):
                                for k in range(KC):
                                    mm = t.matmul(psV[:, ts2, :], wv_b[:, sl, k, :],
                                             xn[:, k, ts2 * 512:(ts2 + 1) * 512],
                                             start=(k == 0), stop=(k == KC - 1))
                            mm.then_inc(pe.h, 1)
                        t_vmm = pe.inc()
                        prev_use["v%d" % sl] = t_vmm

                        @block.tensor
                        def _(t, sl=sl, t_wq=t_wq, wqs=wqs):
                            t.wait_ge(wqs.h, t_wq)
                            if evQ[0] > 0:
                                t.wait_ge(dv.h, evQ[0])
                            for ts2 in range(TSUB):
                                for k in range(KC):
                                    mm = t.matmul(psQ[:, ts2, :], wq_b[:, sl, k, :],
                                             xn[:, k, ts2 * 512:(ts2 + 1) * 512],
                                             start=(k == 0), stop=(k == KC - 1))
                            mm.then_inc(pe.h, 1)
                        t_qmm = pe.inc()
                        prev_use["q%d" % sl] = t_qmm

                        @block.vector
                        def _(v, t_kmm=t_kmm):
                            v.wait_ge(pe.h, t_kmm)
                            v.tensor_copy(kscr[:], psK.ap().rearrange("p a b -> p (a b)")).then_inc(dv.h, 1)
                        evK[0] = dv.inc()

                        @block.vector
                        def _(v, t_vmm=t_vmm, t_k=evK[0]):
                            v.wait_ge(pe.h, t_vmm)
                            v.wait_ge(dv.h, t_k)
                            v.tensor_tensor(kvscr[:], psV.ap().rearrange("p a b -> p (a b)"),
                                            kscr[:], OP.mult).then_inc(dv.h, 1)
                        evV[0] = dv.inc()

                        @block.vector
                        def _(v, eb=eb):
                            vsync(v)
                            v.tensor_tensor_scan(kvtscr[:], ones[:], kvscr[:],
                                                 carry[:, eb:eb + 1], OP.mult, OP.add)
                            vsync(v)
                            v.tensor_copy(carry[:, eb:eb + 1], kvtscr[:, MT - 1:MT])
                        t_scan = dv.n

                        @block.vector
                        def _(v, eb=eb, t_qmm=t_qmm, t_scan=t_scan):
                            v.wait_ge(pe.h, t_qmm)
                            v.wait_ge(dv.h, t_scan)
                            v.scalar_tensor_tensor(vhat[:, eb, :],
                                                   psQ.ap().rearrange("p a b -> p (a b)"),
                                                   DENOM, kvtscr[:], OP.mult,
                                                   OP.mult).then_inc(dv.h, 1)
                        evQ[0] = dv.inc()
                        vhat_tags.append(evQ[0])
                    qkv_done_pe = pe.n

                with ExitStack() as ps2:
                    psO = ps2.enter_context(nc.psum_tensor("psO%d" % mt, [128, 4, 512], F32))
                    wo_b = ps2.enter_context(nc.sbuf_tensor("wo_b%d" % mt, [128, 2, 4, D], BF16))
                    prev_wo = [qkv_done_pe, qkv_done_pe]
                    prev_wo_dv = [vhat_tags[-1], vhat_tags[-1]]
                    ev_o = [0, 0, 0, 0]
                    prev_g_ev = [0]
                    for g in range(4):
                        sl = g % 2
                        t_wo = planw.load(ws[sl], wo_b[:, sl], wo_t[g],
                                          war=[(pe, prev_wo[sl]), (dv, prev_wo_dv[sl])])
                        prev_wo_dv[sl] = 0
                        for tb in range(TB):
                            for fc in range(HC):
                                idx = tb * HC + fc
                                psl = [0, 2, 1, 3][idx % 4]

                                @block.tensor
                                def _(t, g=g, sl=sl, tb=tb, fc=fc, psl=psl,
                                      t_wo=t_wo, idx=idx):
                                    if idx == 0:
                                        t.wait_ge(ws[sl].h, t_wo)
                                        t.wait_ge(dv.h, vhat_tags[-1] if g == 0
                                                  else vhat_tags[g * 4 + 3])
                                    if ev_o[psl] > 0:
                                        t.wait_ge(dv.h, ev_o[psl])
                                    mm = None
                                    for k in range(4):
                                        mm = t.matmul(psO[:, psl, :],
                                                 vhat[:, g * 4 + k, tb * 128:(tb + 1) * 128],
                                                 wo_b[:, sl, k, fc * 512:(fc + 1) * 512],
                                                 start=(k == 0), stop=(k == 3))
                                    mm.then_inc(pe.h, 1)
                                t_omm = pe.inc()

                                @block.vector
                                def _(v, tb=tb, fc=fc, psl=psl, t_omm=t_omm,
                                      idx=idx, pg=prev_g_ev[0]):
                                    v.wait_ge(pe.h, t_omm)
                                    if idx == 0 and pg > 0:
                                        v.wait_ge(dv.h, pg)
                                    v.tensor_tensor(acc[:, tb, fc * 512:(fc + 1) * 512],
                                                    psO[:, psl, :],
                                                    acc[:, tb, fc * 512:(fc + 1) * 512],
                                                    OP.add).then_inc(dv.h, 1)
                                ev_o[psl] = dv.inc()
                        prev_g_ev[0] = dv.n
                        prev_wo[sl] = pe.n
                    fence()
            if stop_after == "attn%d" % mt:
                finish()
                return nc

            with ExitStack() as ps:
                psGm = ps.enter_context(nc.psum_tensor("psGm%d" % mt, [128, 2, 512], F32))
                psUm = ps.enter_context(nc.psum_tensor("psUm%d" % mt, [128, 2, 512], F32))
                psDm = ps.enter_context(nc.psum_tensor("psDm%d" % mt, [128, 4, 512], F32))
                wg_b = ps.enter_context(nc.sbuf_tensor("wg_b%d" % mt, [128, 2, KC, 128], BF16))
                wu_b = ps.enter_context(nc.sbuf_tensor("wu_b%d" % mt, [128, 2, KC, 128], BF16))
                wd_b = ps.enter_context(nc.sbuf_tensor("wd_b%d" % mt, [128, 2, FMC, 512], BF16))
                wtd = ps.enter_context(nc.sbuf_tensor("wtd%d" % mt, [128, FMC, MT], BF16))
                scrm = ps.enter_context(nc.sbuf_tensor("scrm%d" % mt, [128, MT], F32))
                p0 = pe.n
                prev_g = [p0, p0]; prev_u = [p0, p0]; prev_d = [p0, p0]
                wtd_tags = [0] * FMC
                ev_gm = [0]; ev_um = [0]; ev_dm = [0, 0, 0, 0]
                sc_silu = [0]; dv_wtd = [0]; prev_down = [0]; prev_e_ev = [0]
                for e in range(E):
                    for fb in range(FMC):
                        sl = fb % 2
                        wgs, wus = ws[0 + sl], ws[2 + sl]
                        t_wg = planw.load(wgs, wg_b[:, sl], wg_t[e, fb],
                                          war=[(pe, prev_g[sl])])
                        t_wu = planw.load(wus, wu_b[:, sl], wu_t[e, fb],
                                          war=[(pe, prev_u[sl])])

                        @block.tensor
                        def _(t, sl=sl, t_wg=t_wg, wgs=wgs):
                            t.wait_ge(wgs.h, t_wg)
                            if ev_gm[0] > 0:
                                t.wait_ge(sc.h, ev_gm[0])
                            for ts2 in range(TSUB):
                                for k in range(KC):
                                    mm = t.matmul(psGm[:, ts2, :], wg_b[:, sl, k, :],
                                             xn[:, k, ts2 * 512:(ts2 + 1) * 512],
                                             start=(k == 0), stop=(k == KC - 1))
                            mm.then_inc(pe.h, 1)
                        t_gm = pe.inc()
                        prev_g[sl] = t_gm

                        @block.tensor
                        def _(t, sl=sl, t_wu=t_wu, wus=wus):
                            t.wait_ge(wus.h, t_wu)
                            if ev_um[0] > 0:
                                t.wait_ge(dv.h, ev_um[0])
                            for ts2 in range(TSUB):
                                for k in range(KC):
                                    mm = t.matmul(psUm[:, ts2, :], wu_b[:, sl, k, :],
                                             xn[:, k, ts2 * 512:(ts2 + 1) * 512],
                                             start=(k == 0), stop=(k == KC - 1))
                            mm.then_inc(pe.h, 1)
                        t_um = pe.inc()
                        prev_u[sl] = t_um

                        @block.scalar
                        def _(s, sl=sl, t_gm=t_gm):
                            s.wait_ge(pe.h, t_gm)
                            if dv_wtd[0] > 0:
                                s.wait_ge(dv.h, dv_wtd[0])
                            s.activation(scrm[:],
                                         psGm.ap().rearrange("p a b -> p (a b)"),
                                         AF.Silu).then_inc(sc.h, 1)
                        sc_silu[0] = sc.inc()
                        ev_gm[0] = sc_silu[0]

                        @block.vector
                        def _(v, fb=fb, t_um=t_um, t_si=sc_silu[0], pdn=prev_down[0]):
                            v.wait_ge(pe.h, t_um)
                            v.wait_ge(sc.h, t_si)
                            if pdn > 0:
                                v.wait_ge(pe.h, pdn)
                            v.tensor_tensor(wtd[:, fb, :],
                                            psUm.ap().rearrange("p a b -> p (a b)"),
                                            scrm[:], OP.mult).then_inc(dv.h, 1)
                        dv_wtd[0] = dv.inc()
                        ev_um[0] = dv_wtd[0]
                        wtd_tags[fb] = dv_wtd[0]
                    wtd_ready = dv_wtd[0]

                    for hc in range(HC):
                        sl = hc % 2
                        wds = ws[4 + sl]
                        t_wd = planw.load(wds, wd_b[:, sl], wd_t[e, hc],
                                          war=[(pe, prev_d[sl])])
                        for tb in range(TB):
                            psl = [0, 2, 1, 3][(hc * TB + tb) % 4]

                            @block.tensor
                            def _(t, hc=hc, sl=sl, tb=tb, psl=psl, t_wd=t_wd,
                                  wtags=list(wtd_tags)):
                                if tb == 0:
                                    t.wait_ge(ws[4 + hc % 2].h, t_wd)
                                if ev_dm[psl] > 0:
                                    t.wait_ge(dv.h, ev_dm[psl])
                                first = (hc == 0 and tb == 0)
                                for k in range(FMC):
                                    if first:
                                        t.wait_ge(dv.h, wtags[k])
                                    mm = t.matmul(psDm[:, psl, :],
                                             wtd[:, k, tb * 128:(tb + 1) * 128],
                                             wd_b[:, sl, k, :],
                                             start=(k == 0), stop=(k == FMC - 1))
                                mm.then_inc(pe.h, 1)
                            t_dm = pe.inc()

                            @block.vector
                            def _(v, e=e, hc=hc, tb=tb, psl=psl, t_dm=t_dm,
                                  pg=prev_e_ev[0]):
                                v.wait_ge(pe.h, t_dm)
                                if hc == 0 and tb == 0 and pg > 0:
                                    v.wait_ge(dv.h, pg)
                                v.scalar_tensor_tensor(acc[:, tb, hc * 512:(hc + 1) * 512],
                                                       psDm[:, psl, :],
                                                       comb[:, tb, e:e + 1],
                                                       acc[:, tb, hc * 512:(hc + 1) * 512],
                                                       OP.mult, OP.add).then_inc(dv.h, 1)
                            ev_dm[psl] = dv.inc()
                        prev_d[sl] = pe.n
                    prev_down[0] = pe.n
                    prev_e_ev[0] = dv.n
                fence()
            if stop_after == "moe%d" % mt:
                finish()
                return nc

            with ExitStack() as ps:
                psS = ps.enter_context(nc.psum_tensor("psS%d" % mt, [128, 2, 512], F32))
                psU2 = ps.enter_context(nc.psum_tensor("psU2%d" % mt, [128, 2, 512], F32))
                psD2 = ps.enter_context(nc.psum_tensor("psD2%d" % mt, [128, 4, 512], F32))
                sg_bb = ps.enter_context(nc.sbuf_tensor("sg_bb%d" % mt, [128, 2, KC, CBW], BF16))
                su_bb = ps.enter_context(nc.sbuf_tensor("su_bb%d" % mt, [128, 2, KC, CBW], BF16))
                sd_bb = ps.enter_context(nc.sbuf_tensor("sd_bb%d" % mt, [128, 2, 2, D], BF16))
                y_sb = ps.enter_context(nc.sbuf_tensor("y_sb%d" % mt, [128, 2, 2, MT], BF16))
                scrs = ps.enter_context(nc.sbuf_tensor("scrs%d" % mt, [128, MT], F32))
                p0 = pe.n
                prev_sg = [p0, p0]; prev_su = [p0, p0]; prev_sd = [p0, p0]
                prev_y = [p0, p0]
                ev_s = [0]; ev_u2 = [0]; ev_d2 = [0, 0, 0, 0]
                sc_silu2 = [0]; dv_y = [0]; prev_p_ev = [0]
                y_tags = [[0, 0], [0, 0]]
                t_sdw_pair = [0, 0]
                for cb in range(NCB):
                    sl = cb % 2
                    t_sgw = planw.load(ws[0 + sl], sg_bb[:, sl], swg_t[cb],
                                       war=[(pe, prev_sg[sl])])
                    t_suw = planw.load(ws[2 + sl], su_bb[:, sl], swu_t[cb],
                                       war=[(pe, prev_su[sl])])
                    t_sdw_pair[sl] = planw.load(ws[4 + sl], sd_bb[:, sl], swd_t[cb],
                                                war=[(pe, prev_sd[sl])])
                    for ch in range(2):
                        @block.tensor
                        def _(t, sl=sl, ch=ch, t_sgw=t_sgw):
                            if ch == 0:
                                t.wait_ge(ws[0 + sl].h, t_sgw)
                            if ev_s[0] > 0:
                                t.wait_ge(sc.h, ev_s[0])
                            for ts2 in range(TSUB):
                                for k in range(KC):
                                    mm = t.matmul(psS[:, ts2, :],
                                             sg_bb[:, sl, k, ch * 128:(ch + 1) * 128],
                                             xn[:, k, ts2 * 512:(ts2 + 1) * 512],
                                             start=(k == 0), stop=(k == KC - 1))
                            mm.then_inc(pe.h, 1)
                        t_sm = pe.inc()
                        prev_sg[sl] = t_sm

                        @block.tensor
                        def _(t, sl=sl, ch=ch, t_suw=t_suw):
                            if ch == 0:
                                t.wait_ge(ws[2 + sl].h, t_suw)
                            if ev_u2[0] > 0:
                                t.wait_ge(dv.h, ev_u2[0])
                            for ts2 in range(TSUB):
                                for k in range(KC):
                                    mm = t.matmul(psU2[:, ts2, :],
                                             su_bb[:, sl, k, ch * 128:(ch + 1) * 128],
                                             xn[:, k, ts2 * 512:(ts2 + 1) * 512],
                                             start=(k == 0), stop=(k == KC - 1))
                            mm.then_inc(pe.h, 1)
                        t_um2 = pe.inc()
                        prev_su[sl] = t_um2

                        @block.scalar
                        def _(s, t_sm=t_sm):
                            s.wait_ge(pe.h, t_sm)
                            if dv_y[0] > 0:
                                s.wait_ge(dv.h, dv_y[0])
                            s.activation(scrs[:],
                                         psS.ap().rearrange("p a b -> p (a b)"),
                                         AF.Silu).then_inc(sc.h, 1)
                        sc_silu2[0] = sc.inc()
                        ev_s[0] = sc_silu2[0]

                        @block.vector
                        def _(v, sl=sl, ch=ch, t_um2=t_um2, t_si=sc_silu2[0],
                              pv=prev_y[sl]):
                            v.wait_ge(pe.h, t_um2)
                            v.wait_ge(sc.h, t_si)
                            if pv > 0 and ch == 0:
                                v.wait_ge(pe.h, pv)
                            v.tensor_tensor(y_sb[:, sl, ch, :],
                                            psU2.ap().rearrange("p a b -> p (a b)"),
                                            scrs[:], OP.mult).then_inc(dv.h, 1)
                        dv_y[0] = dv.inc()
                        ev_u2[0] = dv_y[0]
                        y_tags[sl][ch] = dv_y[0]

                    if cb % 2 == 1:
                        pair_ready = max(y_tags[0] + y_tags[1])
                        for tb in range(TB):
                            for fc in range(HC):
                                idx = tb * HC + fc
                                psl = idx % 2

                                @block.tensor
                                def _(t, tb=tb, fc=fc, psl=psl, idx=idx,
                                      pair_ready=pair_ready,
                                      t_sd0=t_sdw_pair[0], t_sd1=t_sdw_pair[1]):
                                    if idx == 0:
                                        t.wait_ge(ws[4].h, t_sd0)
                                        t.wait_ge(ws[5].h, t_sd1)
                                        t.wait_ge(dv.h, pair_ready)
                                    if ev_d2[psl] > 0:
                                        t.wait_ge(dv.h, ev_d2[psl])
                                    for k in range(4):
                                        slk, chk = (0, k) if k < 2 else (1, k - 2)
                                        mm = t.matmul(psD2[:, psl, :],
                                                 y_sb[:, slk, chk, tb * 128:(tb + 1) * 128],
                                                 sd_bb[:, slk, chk, fc * 512:(fc + 1) * 512],
                                                 start=(k == 0), stop=(k == 3))
                                    mm.then_inc(pe.h, 1)
                                t_d2 = pe.inc()

                                @block.vector
                                def _(v, tb=tb, fc=fc, psl=psl, t_d2=t_d2,
                                      idx=idx, pg=prev_p_ev[0]):
                                    v.wait_ge(pe.h, t_d2)
                                    if idx == 0 and pg > 0:
                                        v.wait_ge(dv.h, pg)
                                    v.scalar_tensor_tensor(acc[:, tb, fc * 512:(fc + 1) * 512],
                                                           psD2[:, psl, :],
                                                           gt[:, tb, 0:1],
                                                           acc[:, tb, fc * 512:(fc + 1) * 512],
                                                           OP.mult, OP.add).then_inc(dv.h, 1)
                                ev_d2[psl] = dv.inc()
                        prev_p_ev[0] = dv.n
                        prev_y = [pe.n, pe.n]
                        prev_sd = [pe.n, pe.n]
                fence()
            if stop_after == "shared%d" % mt:
                finish()
                return nc

            last_ev = dv.n
            t_st = planx.load(
                qs, out[tok0:tok0 + MT, :].rearrange("(a p) d -> p a d", p=128),
                acc[:], war=[(dv, last_ev), (qs, qs.n - 16)])
            store_tags.append(t_st)

        finish()

    return nc



def _tile_lhsT_cols(wT, nblk, blkw):
    K = wT.shape[0]
    kc = K // 128
    a = wT.reshape(kc, 128, nblk, blkw)
    return np.ascontiguousarray(a.transpose(2, 1, 0, 3))


def _tile_rhs_rows(w, nblk, kper, nw):
    a = w.reshape(nblk, kper, 128, nw)
    return np.ascontiguousarray(a.transpose(0, 2, 1, 3))


_NC_CACHE = {}


def kernel(L0_attn_x, L0_attn_Wq, L0_attn_Wk, L0_attn_Wv, L0_attn_Wo,
           L0_ffn_router_weight, L0_ffn_w_gate, L0_ffn_w_up, L0_ffn_w_down,
           L0_ffn_sw_gate, L0_ffn_sw_up, L0_ffn_sw_down, L0_ffn_sgate_w):
    x = np.ascontiguousarray(np.asarray(L0_attn_x, dtype=np.float32))
    Wq = np.asarray(L0_attn_Wq, dtype=np.float32)
    Wk = np.asarray(L0_attn_Wk, dtype=np.float32)
    Wv = np.asarray(L0_attn_Wv, dtype=np.float32)
    Wo = np.asarray(L0_attn_Wo, dtype=np.float32)
    rw = np.asarray(L0_ffn_router_weight, dtype=np.float32)
    wg = np.asarray(L0_ffn_w_gate, dtype=np.float32)
    wu = np.asarray(L0_ffn_w_up, dtype=np.float32)
    wd = np.asarray(L0_ffn_w_down, dtype=np.float32)
    swg = np.asarray(L0_ffn_sw_gate, dtype=np.float32)
    swu = np.asarray(L0_ffn_sw_up, dtype=np.float32)
    swd = np.asarray(L0_ffn_sw_down, dtype=np.float32)
    sgw = np.asarray(L0_ffn_sgate_w, dtype=np.float32)

    feeds = dict(
        wq_t=_tile_lhsT_cols(np.ascontiguousarray(Wq.T), KC, 128),
        wk_t=_tile_lhsT_cols(np.ascontiguousarray(Wk.T), KC, 128),
        wv_t=_tile_lhsT_cols(np.ascontiguousarray(Wv.T), KC, 128),
        wo_t=_tile_rhs_rows(np.ascontiguousarray(Wo.T), 4, 4, D),
        rw_t=_tile_lhsT_cols(np.ascontiguousarray(rw.T), 1, E)[0],
        sgw_t=_tile_lhsT_cols(np.ascontiguousarray(sgw), 1, 1)[0],
        wg_t=np.stack([_tile_lhsT_cols(wg[e], FMC, 128) for e in range(E)]),
        wu_t=np.stack([_tile_lhsT_cols(wu[e], FMC, 128) for e in range(E)]),
        wd_t=np.stack([np.ascontiguousarray(
            wd[e].reshape(FMC, 128, HC, 512).transpose(2, 1, 0, 3)) for e in range(E)]),
        swg_t=_tile_lhsT_cols(swg, NCB, CBW),
        swu_t=_tile_lhsT_cols(swu, NCB, CBW),
        swd_t=_tile_rhs_rows(swd, NCB, 2, D),
        ident=np.eye(128, dtype=ml_dtypes.bfloat16),
        identf=np.eye(128, dtype=np.float32),
    )

    if "nc" not in _NC_CACHE:
        _NC_CACHE["nc"] = build_nc()
    nc = _NC_CACHE["nc"]

    in_maps = []
    for c in range(8):
        b, h = c // 2, c % 2
        m = dict(feeds)
        m["x_own"] = np.ascontiguousarray(x[b, h * TCORE:(h + 1) * TCORE, :])
        m["x_pre"] = np.ascontiguousarray(x[b, 0:TCORE, :])
        m["flag"] = np.full((128, 1), float(h), dtype=np.float32)
        in_maps.append(m)

    res = run_bass_kernel_spmd(nc, in_maps, core_ids=list(range(8)))
    full = np.empty((4, 4096, D), dtype=np.float32)
    for c in range(8):
        b, h = c // 2, c % 2
        full[b, h * TCORE:(h + 1) * TCORE, :] = np.asarray(res.results[c]["out"])
    return full


# version 11
# speedup vs baseline: 1.0936x; 1.0083x over previous
import numpy as np
import ml_dtypes
from contextlib import ExitStack

import concourse.bass as bass
import concourse.mybir as mybir
from concourse.bass_utils import run_bass_kernel_spmd

F32 = mybir.dt.float32
BF16 = mybir.dt.bfloat16
AF = mybir.ActivationFunctionType
OP = mybir.AluOpType
AX = mybir.AxisListType

D = 2048
KC = 16
TCORE = 2048
MT = 1024
NMT = TCORE // MT
TSUB = MT // 512
TB = MT // 128
E = 32
FMC = 6
HC = 4
FS = 4096
CBW = 256
NCB = FS // CBW
DENOM = float(np.sqrt(np.float32(2048.0)) / 4096.0)
EPS = 1e-5


class S:
    def __init__(self, h):
        self.h = h
        self.n = 0

    def inc(self, k=1):
        self.n += k
        return self.n


class Plan:
    def __init__(self):
        self.items = []

    def load(self, sem, out_ap, in_ap, war=None):
        self.items.append((sem, out_ap, in_ap, list(war or [])))
        return sem.inc(16)


def build_nc(stop_after=None, debug=False, pref_ebs=KC):
    nc = bass.Bass("TRN2", target_bir_lowering=False)

    x_own = nc.dram_tensor("x_own", [TCORE, D], F32, kind="ExternalInput").ap()
    x_pre = nc.dram_tensor("x_pre", [TCORE, D], F32, kind="ExternalInput").ap()
    flag = nc.dram_tensor("flag", [128, 1], F32, kind="ExternalInput").ap()
    ident = nc.dram_tensor("ident", [128, 128], BF16, kind="ExternalInput").ap()
    identf = nc.dram_tensor("identf", [128, 128], F32, kind="ExternalInput").ap()
    wq_t = nc.dram_tensor("wq_t", [KC, 128, KC, 128], F32, kind="ExternalInput").ap()
    wk_t = nc.dram_tensor("wk_t", [KC, 128, KC, 128], F32, kind="ExternalInput").ap()
    wv_t = nc.dram_tensor("wv_t", [KC, 128, KC, 128], F32, kind="ExternalInput").ap()
    wo_t = nc.dram_tensor("wo_t", [4, 128, 4, D], F32, kind="ExternalInput").ap()
    rw_t = nc.dram_tensor("rw_t", [128, KC, E], F32, kind="ExternalInput").ap()
    sgw_t = nc.dram_tensor("sgw_t", [128, KC, 1], F32, kind="ExternalInput").ap()
    wg_t = nc.dram_tensor("wg_t", [E, FMC, 128, KC, 128], F32, kind="ExternalInput").ap()
    wu_t = nc.dram_tensor("wu_t", [E, FMC, 128, KC, 128], F32, kind="ExternalInput").ap()
    wd_t = nc.dram_tensor("wd_t", [E, HC, 128, FMC, 512], F32, kind="ExternalInput").ap()
    swg_t = nc.dram_tensor("swg_t", [NCB, 128, KC, CBW], F32, kind="ExternalInput").ap()
    swu_t = nc.dram_tensor("swu_t", [NCB, 128, KC, CBW], F32, kind="ExternalInput").ap()
    swd_t = nc.dram_tensor("swd_t", [NCB, 128, 2, D], F32, kind="ExternalInput").ap()
    out = nc.dram_tensor("out", [TCORE, D], F32, kind="ExternalOutput").ap()
    if debug:
        d_xn = nc.dram_tensor("d_xn", [128, KC, MT], BF16, kind="ExternalOutput").ap()
        d_carry = nc.dram_tensor("d_carry", [128, KC], F32, kind="ExternalOutput").ap()
        d_pref = nc.dram_tensor("d_pref", [128, KC], F32, kind="ExternalOutput").ap()
        d_comb = nc.dram_tensor("d_comb", [128, TB, E], F32, kind="ExternalOutput").ap()
        d_gt = nc.dram_tensor("d_gt", [128, TB], F32, kind="ExternalOutput").ap()
        d_acc = nc.dram_tensor("d_acc", [128, TB, D], F32, kind="ExternalOutput").ap()

    with ExitStack() as es:
        acc = es.enter_context(nc.sbuf_tensor("acc", [128, TB, D], F32))
        xn = es.enter_context(nc.sbuf_tensor("xn", [128, KC, MT], BF16))
        ones = es.enter_context(nc.sbuf_tensor("ones", [128, MT], F32))
        carry = es.enter_context(nc.sbuf_tensor("carry", [128, KC], F32))
        pref = es.enter_context(nc.sbuf_tensor("pref", [128, KC], F32))
        flag_sb = es.enter_context(nc.sbuf_tensor("flag_sb", [128, 1], F32))
        id_sb = es.enter_context(nc.sbuf_tensor("id_sb", [128, 128], BF16))
        idf_sb = es.enter_context(nc.sbuf_tensor("idf_sb", [128, 128], F32))
        ones1 = es.enter_context(nc.sbuf_tensor("ones1", [1, 128], F32))
        comb_fm = es.enter_context(nc.sbuf_tensor("comb_fm", [32, MT], F32))
        comb = es.enter_context(nc.sbuf_tensor("comb", [128, TB, E], F32))
        gt = es.enter_context(nc.sbuf_tensor("gt", [128, TB, 1], F32))
        cur2 = {}
        cur = {}
        lg = es.enter_context(nc.sbuf_tensor("lg", [128, E], F32))
        eee = es.enter_context(nc.sbuf_tensor("eee", [128, E], F32))
        msk = es.enter_context(nc.sbuf_tensor("msk", [128, E], F32))
        top8 = es.enter_context(nc.sbuf_tensor("top8", [128, 8], F32))
        s_a = es.enter_context(nc.sbuf_tensor("s_a", [128, 1], F32))
        s_b = es.enter_context(nc.sbuf_tensor("s_b", [128, 1], F32))
        s_c = es.enter_context(nc.sbuf_tensor("s_c", [128, 1], F32))
        s_d = es.enter_context(nc.sbuf_tensor("s_d", [128, 1], F32))
        s_e = es.enter_context(nc.sbuf_tensor("s_e", [128, 1], F32))
        eps_sb = es.enter_context(nc.sbuf_tensor("eps_sb", [128, 1], F32))

        pe = S(es.enter_context(nc.semaphore("pe")))
        dv = S(es.enter_context(nc.semaphore("dv")))
        sc = S(es.enter_context(nc.semaphore("sc")))
        ws = [S(es.enter_context(nc.semaphore("w%d" % i))) for i in range(6)]
        qc = S(es.enter_context(nc.semaphore("qc")))
        xa = S(es.enter_context(nc.semaphore("xa")))
        xb = S(es.enter_context(nc.semaphore("xb")))
        xc = S(es.enter_context(nc.semaphore("xc")))
        qs = S(es.enter_context(nc.semaphore("qs")))
        dma_sems = ws + [qc, xa, xb, xc, qs]

        block = es.enter_context(nc.Block())
        planw = Plan()
        planx = Plan()

        def emit_plan(q, plan):
            for dsem, out_ap, in_ap, war in plan.items:
                for (sem, val) in war:
                    if val > 0:
                        q.wait_ge(sem.h, val)
                q.dma_start(out=out_ap, in_=in_ap).then_inc(dsem.h, 16)
            plan.items = []

        def vsync(v):
            v.drain()
            v.sem_inc(dv.h, 1)
            dv.inc()
            v.wait_ge(dv.h, dv.n)

        def ssync(s):
            s.drain()
            s.sem_inc(sc.h, 1)
            sc.inc()
            s.wait_ge(sc.h, sc.n)

        def barrier():
            @block.gpsimd
            def _(g):
                emit_plan(g, planw)

            @block.sync
            def _(sy):
                emit_plan(sy, planx)

            @block.tensor
            def _(t):
                t.drain()
                t.sem_inc(pe.h, 1)
            pe.inc()

            @block.vector
            def _(v):
                v.drain()
                v.sem_inc(dv.h, 1)
            dv.inc()

            @block.scalar
            def _(s):
                s.drain()
                s.sem_inc(sc.h, 1)
            sc.inc()
            targets = ([(pe, pe.n), (dv, dv.n), (sc, sc.n)]
                       + [(s2, s2.n) for s2 in dma_sems])

            @block.tensor
            def _(t):
                for sem, val in targets:
                    if sem is not pe:
                        t.wait_ge(sem.h, val)

            @block.vector
            def _(v):
                for sem, val in targets:
                    if sem is not dv:
                        v.wait_ge(sem.h, val)

            @block.scalar
            def _(s):
                for sem, val in targets:
                    if sem is not sc:
                        s.wait_ge(sem.h, val)

            @block.gpsimd
            def _(g):
                for sem, val in targets:
                    g.wait_ge(sem.h, val)

            @block.sync
            def _(sy):
                for sem, val in targets:
                    sy.wait_ge(sem.h, val)

        def finish():
            @block.gpsimd
            def _(g):
                emit_plan(g, planw)

            @block.sync
            def _(sy):
                emit_plan(sy, planx)
                if debug:
                    sy.wait_ge(dv.h, dv.n)
                    sy.wait_ge(sc.h, sc.n)
                    sy.dma_start(out=d_xn, in_=xn[:]).then_inc(qs.h, 16)
                    sy.dma_start(out=d_carry, in_=carry[:]).then_inc(qs.h, 16)
                    sy.dma_start(out=d_pref, in_=pref[:]).then_inc(qs.h, 16)
                    sy.dma_start(out=d_comb, in_=comb[:]).then_inc(qs.h, 16)
                    sy.dma_start(out=d_gt, in_=gt.ap().rearrange("p a b -> p (a b)")).then_inc(qs.h, 16)
                    sy.dma_start(out=d_acc, in_=acc[:]).then_inc(qs.h, 16)
                    qs.inc(16 * 6)
                for s2 in dma_sems:
                    sy.wait_ge(s2.h, s2.n)

            @block.vector
            def _(v):
                for s2 in dma_sems:
                    v.wait_ge(s2.h, s2.n)

        def fence():
            @block.gpsimd
            def _(g):
                emit_plan(g, planw)

            @block.sync
            def _(sy):
                emit_plan(sy, planx)
            pn, dn, sn = pe.n, dv.n, sc.n

            @block.tensor
            def _(t):
                t.wait_ge(dv.h, dn)
                t.wait_ge(sc.h, sn)

            @block.vector
            def _(v):
                v.wait_ge(pe.h, pn)
                v.wait_ge(sc.h, sn)

            @block.scalar
            def _(s):
                s.wait_ge(pe.h, pn)
                s.wait_ge(dv.h, dn)

        planx.load(qc, id_sb[:], ident)
        planx.load(qc, idf_sb[:], identf)
        t_consts = planx.load(qc, flag_sb[:], flag)

        @block.vector
        def _(v):
            v.memset(ones[:], 1.0)
            v.memset(pref[:], 0.0)
            v.memset(carry[:], 0.0)
            v.memset(eps_sb[:], EPS)
            v.memset(ones1[:], 1.0)
            if debug:
                v.memset(acc[:], 0.0)
                v.memset(comb[:], 0.0)
                v.memset(gt[:], 0.0)
                v.memset(xn[:], 0.0)
            v.drain()
            v.sem_inc(dv.h, 1)
        dv.inc()

        if stop_after == "init":
            finish()
            return nc

        def ln_emit(chunks, psT, dst_fm, router=None):
            n = len(chunks)
            xn_tag = [0] * n; sq_tag = [0] * n; var_tag = [0] * n
            trlast = [0] * n; cplast = [0] * n
            cp_slot = [0, 0]
            rmm_tag = [0] * n; gmm_tag = [0] * n; lgcp = [0] * n; sig = [0] * n
            comb_tag = [0] * n; cfcp = [0] * n

            def chain(t):
                ch = chunks[t]
                src = ch["src"]
                if "load" in ch:
                    war = []
                    if t >= 3:
                        war = [(dv, xn_tag[t - 3]), (sc, sq_tag[t - 3])]
                    ready = ch["load"](war)
                else:
                    ready = ch["ready"]
                par = t % 3
                s2a, s2b, s2c, s2d, xn_tm2 = (cur2["s2a"], cur2["s2b"],
                                              cur2["s2c"], cur2["s2d"],
                                              cur2["xn3"])
                sa = s2a[:, par:par + 1]; sb = s2b[:, par:par + 1]
                sc_ = s2c[:, par:par + 1]; sd = s2d[:, par:par + 1]

                @block.scalar
                def _(s):
                    for sem, val in ready:
                        if val > 0:
                            s.wait_ge(sem.h, val)
                    if t >= 3:
                        s.wait_ge(dv.h, xn_tag[t - 3])
                    s.activation(cur["scr"][:], src, AF.Identity, accum_out=sa)
                    ssync(s)
                    s.activation(cur["scr"][:], src, AF.Square,
                                 accum_out=sb).then_inc(sc.h, 1)
                sq_tag[t] = sc.inc()

                @block.vector
                def _(v):
                    v.wait_ge(sc.h, sq_tag[t])
                    if t >= 3 and trlast[t - 3] > 0:
                        v.wait_ge(pe.h, trlast[t - 3])
                    v.tensor_scalar(sa, sa, 1.0 / D, None, OP.mult)
                    v.tensor_scalar(sb, sb, 1.0 / D, None, OP.mult)
                    vsync(v)
                    v.tensor_tensor(sc_, sa, sa, OP.mult)
                    vsync(v)
                    v.tensor_tensor(sc_, sb, sc_, OP.subtract)
                    v.drain()
                    v.sem_inc(dv.h, 1)
                var_tag[t] = dv.inc()

                @block.scalar
                def _(s):
                    s.wait_ge(dv.h, var_tag[t])
                    s.activation(sd, sc_, AF.Ln, bias=eps_sb[:, 0:1])
                    ssync(s)
                    s.activation(sd, sd, AF.Exp, scale=-0.5)
                    s.drain()
                    s.sem_inc(sc.h, 1)
                t_sd = sc.inc()

                @block.vector
                def _(v):
                    v.wait_ge(sc.h, t_sd)
                    v.tensor_scalar(xn_tm2[:, par, :], src, sa, sd,
                                    OP.subtract, OP.mult)
                    v.drain()
                    v.sem_inc(dv.h, 1)
                xn_tag[t] = dv.inc()

            def trcopies(t):
                col0 = chunks[t]["col0"]
                par3 = t % 3
                par = t % 2
                xn_tm2 = cur2["xn3"]
                for fb in range(KC):
                    @block.tensor
                    def _(t_, fb=fb):
                        if fb == 0:
                            t_.wait_ge(dv.h, xn_tag[t])
                            t_.wait_ge(qc.h, t_consts)
                            if cp_slot[par] > 0:
                                t_.wait_ge(dv.h, cp_slot[par])
                        t_.transpose(psT[:, par, fb * 128:(fb + 1) * 128],
                                     xn_tm2[:, par3, fb * 128:(fb + 1) * 128],
                                     id_sb[:]).then_inc(pe.h, 1)
                    pe.inc()
                trlast[t] = pe.n

                @block.vector
                def _(v):
                    v.wait_ge(pe.h, trlast[t])
                    v.tensor_copy(dst_fm[:, :, col0:col0 + 128],
                                  psT[:, par, :].rearrange("p (a b) -> p a b",
                                                           a=KC)).then_inc(dv.h, 1)
                cp_slot[par] = dv.inc()
                cplast[t] = dv.n

            def router_pe(t):
                r = router
                par = t % 2

                @block.tensor
                def _(t_):
                    if t == 0:
                        t_.wait_ge(ws[0].h, r["t_rw"])
                        t_.wait_ge(ws[2].h, r["t_sg"])
                    t_.wait_ge(dv.h, cplast[t])
                    if t >= 2:
                        t_.wait_ge(dv.h, lgcp[t - 2])
                        t_.wait_ge(sc.h, sig[t - 2])
                    mm = None
                    for k in range(KC):
                        mm = t_.matmul(r["psR"][:, par, 0:E],
                                       xn[:, k, t * 128:(t + 1) * 128],
                                       r["rw_b"][:, k, :],
                                       start=(k == 0), stop=(k == KC - 1))
                    mm.then_inc(pe.h, 1)
                rmm_tag[t] = pe.inc()

                @block.tensor
                def _(t_):
                    if t >= 2:
                        t_.wait_ge(sc.h, sig[t - 2])
                    mm = None
                    for k in range(KC):
                        mm = t_.matmul(r["psR"][:, par, 32:33],
                                       xn[:, k, t * 128:(t + 1) * 128],
                                       r["sg_b"][:, k, :],
                                       start=(k == 0), stop=(k == KC - 1))
                    mm.then_inc(pe.h, 1)
                gmm_tag[t] = pe.inc()

            def router_rest(t):
                r = router
                par = t % 2

                @block.scalar
                def _(s):
                    s.wait_ge(pe.h, gmm_tag[t])
                    if t >= 1:
                        s.wait_ge(dv.h, sig_dv[t - 1])
                    s.activation(s_e[:], r["psR"][:, par, 32:33], AF.Exp,
                                 scale=-1.0).then_inc(sc.h, 1)
                sig[t] = sc.inc()

                @block.vector
                def _(v):
                    v.wait_ge(sc.h, sig[t])
                    v.tensor_scalar(gt[:, t, :], s_e[:], 1.0, None, OP.add)
                    vsync(v)
                    v.reciprocal(gt[:, t, :], gt[:, t, :])
                    v.drain()
                    v.sem_inc(dv.h, 1)
                sig_dv[t] = dv.inc()

                @block.vector
                def _(v):
                    v.wait_ge(pe.h, rmm_tag[t])
                    v.tensor_copy(lg[:], r["psR"][:, par, 0:E]).then_inc(dv.h, 1)
                    vsync(v)
                    v.tensor_reduce(s_a[:], lg[:], AX.X, OP.max)
                    vsync(v)
                    v.tensor_scalar(s_a[:], s_a[:], -1.0, None, OP.mult)
                    v.drain()
                    v.sem_inc(dv.h, 1)
                lgcp[t] = dv.inc()
                t_nm = dv.inc()

                @block.scalar
                def _(s):
                    s.wait_ge(dv.h, t_nm)
                    s.activation(eee[:], lg[:], AF.Exp, bias=s_a[:],
                                 scale=1.0).then_inc(sc.h, 1)
                t_ex = sc.inc()

                @block.vector
                def _(v):
                    v.wait_ge(sc.h, t_ex)
                    v.max(top8[:], eee[:])
                    vsync(v)
                    v.tensor_reduce(s_b[:], top8[:, 0:4], AX.X, OP.add)
                    vsync(v)
                    v.reciprocal(s_b[:], s_b[:])
                    vsync(v)
                    v.tensor_scalar(msk[:], eee[:], top8[:, 3:4], None, OP.is_ge)
                    vsync(v)
                    v.tensor_tensor(msk[:], eee[:], msk[:], OP.mult)
                    vsync(v)
                    v.tensor_scalar(comb[:, t, :], msk[:], s_b[:, 0:1], None,
                                    OP.mult)
                    v.drain()
                    v.sem_inc(dv.h, 1)
                comb_tag[t] = dv.inc()

            def cfT(t):
                r = router

                @block.tensor
                def _(t_):
                    t_.wait_ge(dv.h, comb_tag[t])
                    if t >= 2:
                        t_.wait_ge(dv.h, cfcp[t - 2])
                    t_.transpose(r["psCF"][0:E, t % 2, 0:128], comb[:, t, :],
                                 idf_sb[:]).then_inc(pe.h, 1)
                t_cf = pe.inc()

                @block.vector
                def _(v):
                    v.wait_ge(pe.h, t_cf)
                    v.tensor_copy(comb_fm[:, t * 128:(t + 1) * 128],
                                  r["psCF"][0:E, t % 2, 0:128]).then_inc(dv.h, 1)
                cfcp[t] = dv.inc()

            sig_dv = [0] * n
            for step in range(n + 5):
                if 2 <= step and step - 2 < n:
                    trcopies(step - 2)
                if step < n:
                    chain(step)
                if router is not None and 3 <= step and step - 3 < n:
                    router_pe(step - 3)
                    router_rest(step - 3)
                if router is not None and 4 <= step and step - 4 < n:
                    cfT(step - 4)
            return dv.n

        for half in range(2):
            with ExitStack() as ps:
                psT = ps.enter_context(nc.psum_tensor("psTp%d" % half, [128, 2, 2048], BF16))
                psK = ps.enter_context(nc.psum_tensor("psKp%d" % half, [128, 2, 512], F32))
                psV = ps.enter_context(nc.psum_tensor("psVp%d" % half, [128, 2, 512], F32))
                xld = ps.enter_context(nc.sbuf_tensor("xldp%d" % half, [128, 3, D], F32))
                cur["scr"] = ps.enter_context(nc.sbuf_tensor("scrp%d" % half, [128, D], F32))
                kscr = ps.enter_context(nc.sbuf_tensor("kscrp%d" % half, [128, MT], F32))
                kvscr = ps.enter_context(nc.sbuf_tensor("kvscrp%d" % half, [128, MT], F32))
                wk_b = ps.enter_context(nc.sbuf_tensor("wkp%d" % half, [128, 2, KC, 128], BF16))
                wv_b = ps.enter_context(nc.sbuf_tensor("wvp%d" % half, [128, 2, KC, 128], BF16))

                base_dv, base_sc = dv.n, sc.n
                cur2["xn3"] = ps.enter_context(nc.sbuf_tensor("xn3p%d" % half, [128, 3, D], BF16))
                cur2["s2a"] = ps.enter_context(nc.sbuf_tensor("s2ap%d" % half, [128, 3], F32))
                cur2["s2b"] = ps.enter_context(nc.sbuf_tensor("s2bp%d" % half, [128, 3], F32))
                cur2["s2c"] = ps.enter_context(nc.sbuf_tensor("s2cp%d" % half, [128, 3], F32))
                cur2["s2d"] = ps.enter_context(nc.sbuf_tensor("s2dp%d" % half, [128, 3], F32))

                def mkload(ch):
                    sl = ch % 3
                    xsem = [xa, xb, xc][sl]
                    r0 = half * MT + ch * 128

                    def load(war):
                        w2 = [(dv, base_dv), (sc, base_sc)] if not war else war
                        t_x = planx.load(xsem, xld[:, sl, :],
                                         x_pre[r0:r0 + 128, :], war=w2)
                        return [(xsem, t_x)]
                    return load
                chunks = [dict(src=xld[:, ch % 3, :], col0=ch * 128,
                               load=mkload(ch)) for ch in range(TB)]
                ln_emit(chunks, psT, xn)
                xn_ready = dv.n
                if stop_after == "prefln%d" % half:
                    barrier()
                    finish()
                    return nc

                p0 = pe.n
                prev_use = {"k0": p0, "k1": p0, "v0": p0, "v1": p0}
                evK, evV = [0], [0]
                for eb in range(pref_ebs):
                    sl = eb % 2
                    wks = ws[0 + sl]
                    wvs = ws[2 + sl]
                    t_wk = planw.load(wks, wk_b[:, sl], wk_t[eb],
                                      war=[(pe, prev_use["k%d" % sl])])
                    t_wv = planw.load(wvs, wv_b[:, sl], wv_t[eb],
                                      war=[(pe, prev_use["v%d" % sl])])

                    @block.tensor
                    def _(t, sl=sl, t_wk=t_wk, eb=eb, wks=wks):
                        t.wait_ge(wks.h, t_wk)
                        if eb == 0:
                            t.wait_ge(dv.h, xn_ready)
                        if evK[0] > 0:
                            t.wait_ge(dv.h, evK[0])
                        for ts2 in range(TSUB):
                            for k in range(KC):
                                mm = t.matmul(psK[:, ts2, :], wk_b[:, sl, k, :],
                                         xn[:, k, ts2 * 512:(ts2 + 1) * 512],
                                         start=(k == 0), stop=(k == KC - 1))
                        mm.then_inc(pe.h, 1)
                    t_kmm = pe.inc()
                    prev_use["k%d" % sl] = t_kmm

                    @block.tensor
                    def _(t, sl=sl, t_wv=t_wv, wvs=wvs):
                        t.wait_ge(wvs.h, t_wv)
                        if evV[0] > 0:
                            t.wait_ge(dv.h, evV[0])
                        for ts2 in range(TSUB):
                            for k in range(KC):
                                mm = t.matmul(psV[:, ts2, :], wv_b[:, sl, k, :],
                                         xn[:, k, ts2 * 512:(ts2 + 1) * 512],
                                         start=(k == 0), stop=(k == KC - 1))
                        mm.then_inc(pe.h, 1)
                    t_vmm = pe.inc()
                    prev_use["v%d" % sl] = t_vmm

                    @block.vector
                    def _(v, t_kmm=t_kmm, t_vmm=t_vmm, eb=eb):
                        v.wait_ge(pe.h, t_kmm)
                        v.tensor_copy(kscr[:], psK.ap().rearrange("p a b -> p (a b)")).then_inc(dv.h, 1)
                    evK[0] = dv.inc()

                    @block.vector
                    def _(v, t_vmm=t_vmm, eb=eb, t_k=evK[0]):
                        v.wait_ge(pe.h, t_vmm)
                        v.wait_ge(dv.h, t_k)
                        v.tensor_tensor(kvscr[:], psV.ap().rearrange("p a b -> p (a b)"),
                                        kscr[:], OP.mult).then_inc(dv.h, 1)
                    evV[0] = dv.inc()

                    @block.vector
                    def _(v, eb=eb):
                        vsync(v)
                        v.tensor_reduce(s_e[:], kvscr[:], AX.X, OP.add)
                        vsync(v)
                        v.tensor_tensor(pref[:, eb:eb + 1], pref[:, eb:eb + 1],
                                        s_e[:], OP.add)
                barrier()
            if stop_after == "prefkv%d" % half:
                finish()
                return nc

        @block.vector
        def _(v):
            v.wait_ge(qc.h, t_consts)
            v.tensor_scalar(carry[:], pref[:], flag_sb[:, 0:1], None, OP.mult)
            v.drain()
            v.sem_inc(dv.h, 1)
        dv.inc()

        if stop_after == "prefix":
            finish()
            return nc

        store_tags = []
        for mt in range(NMT):
            tok0 = mt * MT

            with ExitStack() as ps:
                psT = ps.enter_context(nc.psum_tensor("psT%d" % mt, [128, 2, 2048], BF16))
                psR = ps.enter_context(nc.psum_tensor("psR%d" % mt, [128, 2, 512], F32))
                psCF = ps.enter_context(nc.psum_tensor("psCF%d" % mt, [128, 2, 512], F32))
                cur["scr"] = ps.enter_context(nc.sbuf_tensor("scrl%d" % mt, [128, D], F32))
                rw_b = ps.enter_context(nc.sbuf_tensor("rw_b%d" % mt, [128, KC, E], BF16))
                sg_b = ps.enter_context(nc.sbuf_tensor("sg_b%d" % mt, [128, KC, 1], BF16))
                p0 = pe.n
                t_rw = planw.load(ws[0], rw_b[:], rw_t, war=[(pe, p0)])
                t_sg = planw.load(ws[2], sg_b[:], sgw_t, war=[(pe, p0)])
                cur2["xn3"] = ps.enter_context(nc.sbuf_tensor("xn3l%d" % mt, [128, 3, D], BF16))
                cur2["s2a"] = ps.enter_context(nc.sbuf_tensor("s2al%d" % mt, [128, 3], F32))
                cur2["s2b"] = ps.enter_context(nc.sbuf_tensor("s2bl%d" % mt, [128, 3], F32))
                cur2["s2c"] = ps.enter_context(nc.sbuf_tensor("s2cl%d" % mt, [128, 3], F32))
                cur2["s2d"] = ps.enter_context(nc.sbuf_tensor("s2dl%d" % mt, [128, 3], F32))
                st_war = [(qs, store_tags[-1])] if store_tags else []

                def mkload(tb):
                    xsem = [xa, xb, xc][tb % 3]
                    r0 = tok0 + tb * 128

                    def load(war):
                        t_x = planx.load(xsem, acc[:, tb, :],
                                         x_own[r0:r0 + 128, :],
                                         war=(war or []) + st_war)
                        return [(xsem, t_x)]
                    return load
                router_ctx = dict(psR=psR, psCF=psCF, rw_b=rw_b,
                                  sg_b=sg_b, t_rw=t_rw, t_sg=t_sg)
                chunks = [dict(src=acc[:, tb, :], col0=tb * 128,
                               load=mkload(tb)) for tb in range(TB)]
                ln_emit(chunks, psT, xn, router=router_ctx)
                fence()
            if stop_after == "ln%d" % mt:
                finish()
                return nc
            if stop_after == "router%d" % mt:
                finish()
                return nc

            with ExitStack() as ps:
                vhat = ps.enter_context(nc.sbuf_tensor("vhat%d" % mt, [128, KC, MT], BF16))
                vhat_tags = []
                with ExitStack() as ps2:
                    psK = ps2.enter_context(nc.psum_tensor("psK%d" % mt, [128, 2, 512], F32))
                    psV = ps2.enter_context(nc.psum_tensor("psV%d" % mt, [128, 2, 512], F32))
                    psQ = ps2.enter_context(nc.psum_tensor("psQ%d" % mt, [128, 2, 512], F32))
                    wq_b = ps2.enter_context(nc.sbuf_tensor("wqa%d" % mt, [128, 2, KC, 128], BF16))
                    wk_b = ps2.enter_context(nc.sbuf_tensor("wka%d" % mt, [128, 2, KC, 128], BF16))
                    wv_b = ps2.enter_context(nc.sbuf_tensor("wva%d" % mt, [128, 2, KC, 128], BF16))
                    kscr = ps2.enter_context(nc.sbuf_tensor("kscr%d" % mt, [128, MT], F32))
                    kvscr = ps2.enter_context(nc.sbuf_tensor("kvscr%d" % mt, [128, MT], F32))
                    kvtscr = ps2.enter_context(nc.sbuf_tensor("kvtscr%d" % mt, [128, MT], F32))
                    p0 = pe.n
                    prev_use = {"q0": p0, "q1": p0, "k0": p0, "k1": p0,
                                "v0": p0, "v1": p0}
                    evK, evV, evQ = [0], [0], [0]
                    for eb in range(KC):
                        sl = eb % 2
                        wqs, wks, wvs = ws[0 + sl], ws[2 + sl], ws[4 + sl]
                        t_wq = planw.load(wqs, wq_b[:, sl], wq_t[eb],
                                          war=[(pe, prev_use.get("q%d" % sl, 0))])
                        t_wk = planw.load(wks, wk_b[:, sl], wk_t[eb],
                                          war=[(pe, prev_use.get("k%d" % sl, 0))])
                        t_wv = planw.load(wvs, wv_b[:, sl], wv_t[eb],
                                          war=[(pe, prev_use.get("v%d" % sl, 0))])

                        @block.tensor
                        def _(t, sl=sl, t_wk=t_wk, wks=wks):
                            t.wait_ge(wks.h, t_wk)
                            if evK[0] > 0:
                                t.wait_ge(dv.h, evK[0])
                            for ts2 in range(TSUB):
                                for k in range(KC):
                                    mm = t.matmul(psK[:, ts2, :], wk_b[:, sl, k, :],
                                             xn[:, k, ts2 * 512:(ts2 + 1) * 512],
                                             start=(k == 0), stop=(k == KC - 1))
                            mm.then_inc(pe.h, 1)
                        t_kmm = pe.inc()
                        prev_use["k%d" % sl] = t_kmm

                        @block.tensor
                        def _(t, sl=sl, t_wv=t_wv, wvs=wvs):
                            t.wait_ge(wvs.h, t_wv)
                            if evV[0] > 0:
                                t.wait_ge(dv.h, evV[0])
                            for ts2 in range(TSUB):
                                for k in range(KC):
                                    mm = t.matmul(psV[:, ts2, :], wv_b[:, sl, k, :],
                                             xn[:, k, ts2 * 512:(ts2 + 1) * 512],
                                             start=(k == 0), stop=(k == KC - 1))
                            mm.then_inc(pe.h, 1)
                        t_vmm = pe.inc()
                        prev_use["v%d" % sl] = t_vmm

                        @block.tensor
                        def _(t, sl=sl, t_wq=t_wq, wqs=wqs):
                            t.wait_ge(wqs.h, t_wq)
                            if evQ[0] > 0:
                                t.wait_ge(dv.h, evQ[0])
                            for ts2 in range(TSUB):
                                for k in range(KC):
                                    mm = t.matmul(psQ[:, ts2, :], wq_b[:, sl, k, :],
                                             xn[:, k, ts2 * 512:(ts2 + 1) * 512],
                                             start=(k == 0), stop=(k == KC - 1))
                            mm.then_inc(pe.h, 1)
                        t_qmm = pe.inc()
                        prev_use["q%d" % sl] = t_qmm

                        @block.vector
                        def _(v, t_kmm=t_kmm):
                            v.wait_ge(pe.h, t_kmm)
                            v.tensor_copy(kscr[:], psK.ap().rearrange("p a b -> p (a b)")).then_inc(dv.h, 1)
                        evK[0] = dv.inc()

                        @block.vector
                        def _(v, t_vmm=t_vmm, t_k=evK[0]):
                            v.wait_ge(pe.h, t_vmm)
                            v.wait_ge(dv.h, t_k)
                            v.tensor_tensor(kvscr[:], psV.ap().rearrange("p a b -> p (a b)"),
                                            kscr[:], OP.mult).then_inc(dv.h, 1)
                        evV[0] = dv.inc()

                        @block.vector
                        def _(v, eb=eb):
                            vsync(v)
                            v.tensor_tensor_scan(kvtscr[:], ones[:], kvscr[:],
                                                 carry[:, eb:eb + 1], OP.mult, OP.add)
                            vsync(v)
                            v.tensor_copy(carry[:, eb:eb + 1], kvtscr[:, MT - 1:MT])
                        t_scan = dv.n

                        @block.vector
                        def _(v, eb=eb, t_qmm=t_qmm, t_scan=t_scan):
                            v.wait_ge(pe.h, t_qmm)
                            v.wait_ge(dv.h, t_scan)
                            v.scalar_tensor_tensor(vhat[:, eb, :],
                                                   psQ.ap().rearrange("p a b -> p (a b)"),
                                                   DENOM, kvtscr[:], OP.mult,
                                                   OP.mult).then_inc(dv.h, 1)
                        evQ[0] = dv.inc()
                        vhat_tags.append(evQ[0])
                    qkv_done_pe = pe.n

                with ExitStack() as ps2:
                    psO = ps2.enter_context(nc.psum_tensor("psO%d" % mt, [128, 4, 512], F32))
                    wo_b = ps2.enter_context(nc.sbuf_tensor("wo_b%d" % mt, [128, 2, 4, D], BF16))
                    prev_wo = [qkv_done_pe, qkv_done_pe]
                    prev_wo_dv = [vhat_tags[-1], vhat_tags[-1]]
                    ev_o = [0, 0, 0, 0]
                    prev_g_ev = [0]
                    for g in range(4):
                        sl = g % 2
                        t_wo = planw.load(ws[sl], wo_b[:, sl], wo_t[g],
                                          war=[(pe, prev_wo[sl]), (dv, prev_wo_dv[sl])])
                        prev_wo_dv[sl] = 0
                        for tb in range(TB):
                            for fc in range(HC):
                                idx = tb * HC + fc
                                psl = [0, 2, 1, 3][idx % 4]

                                @block.tensor
                                def _(t, g=g, sl=sl, tb=tb, fc=fc, psl=psl,
                                      t_wo=t_wo, idx=idx):
                                    if idx == 0:
                                        t.wait_ge(ws[sl].h, t_wo)
                                        t.wait_ge(dv.h, vhat_tags[-1] if g == 0
                                                  else vhat_tags[g * 4 + 3])
                                    if ev_o[psl] > 0:
                                        t.wait_ge(dv.h, ev_o[psl])
                                    mm = None
                                    for k in range(4):
                                        mm = t.matmul(psO[:, psl, :],
                                                 vhat[:, g * 4 + k, tb * 128:(tb + 1) * 128],
                                                 wo_b[:, sl, k, fc * 512:(fc + 1) * 512],
                                                 start=(k == 0), stop=(k == 3))
                                    mm.then_inc(pe.h, 1)
                                t_omm = pe.inc()

                                @block.vector
                                def _(v, tb=tb, fc=fc, psl=psl, t_omm=t_omm,
                                      idx=idx, pg=prev_g_ev[0]):
                                    v.wait_ge(pe.h, t_omm)
                                    if idx == 0 and pg > 0:
                                        v.wait_ge(dv.h, pg)
                                    v.tensor_tensor(acc[:, tb, fc * 512:(fc + 1) * 512],
                                                    psO[:, psl, :],
                                                    acc[:, tb, fc * 512:(fc + 1) * 512],
                                                    OP.add).then_inc(dv.h, 1)
                                ev_o[psl] = dv.inc()
                        prev_g_ev[0] = dv.n
                        prev_wo[sl] = pe.n
                    fence()
            if stop_after == "attn%d" % mt:
                finish()
                return nc

            with ExitStack() as ps:
                psGm = ps.enter_context(nc.psum_tensor("psGm%d" % mt, [128, 2, 512], F32))
                psUm = ps.enter_context(nc.psum_tensor("psUm%d" % mt, [128, 2, 512], F32))
                psDm = ps.enter_context(nc.psum_tensor("psDm%d" % mt, [128, 4, 512], F32))
                wg_b = ps.enter_context(nc.sbuf_tensor("wg_b%d" % mt, [128, 2, KC, 128], BF16))
                wu_b = ps.enter_context(nc.sbuf_tensor("wu_b%d" % mt, [128, 2, KC, 128], BF16))
                wd_b = ps.enter_context(nc.sbuf_tensor("wd_b%d" % mt, [128, 2, FMC, 512], BF16))
                wtd = ps.enter_context(nc.sbuf_tensor("wtd%d" % mt, [128, FMC, MT], BF16))
                scrm = ps.enter_context(nc.sbuf_tensor("scrm%d" % mt, [128, MT], F32))
                p0 = pe.n
                prev_g = [p0, p0]; prev_u = [p0, p0]; prev_d = [p0, p0]
                wtd_tags = [0] * FMC
                ev_gm = [0]; ev_um = [0]; ev_dm = [0, 0, 0, 0]
                sc_silu = [0]; dv_wtd = [0]; prev_down = [0]; prev_e_ev = [0]
                for e in range(E):
                    for fb in range(FMC):
                        sl = fb % 2
                        wgs, wus = ws[0 + sl], ws[2 + sl]
                        t_wg = planw.load(wgs, wg_b[:, sl], wg_t[e, fb],
                                          war=[(pe, prev_g[sl])])
                        t_wu = planw.load(wus, wu_b[:, sl], wu_t[e, fb],
                                          war=[(pe, prev_u[sl])])

                        @block.tensor
                        def _(t, sl=sl, t_wg=t_wg, wgs=wgs):
                            t.wait_ge(wgs.h, t_wg)
                            if ev_gm[0] > 0:
                                t.wait_ge(sc.h, ev_gm[0])
                            for ts2 in range(TSUB):
                                for k in range(KC):
                                    mm = t.matmul(psGm[:, ts2, :], wg_b[:, sl, k, :],
                                             xn[:, k, ts2 * 512:(ts2 + 1) * 512],
                                             start=(k == 0), stop=(k == KC - 1))
                            mm.then_inc(pe.h, 1)
                        t_gm = pe.inc()
                        prev_g[sl] = t_gm

                        @block.tensor
                        def _(t, sl=sl, t_wu=t_wu, wus=wus):
                            t.wait_ge(wus.h, t_wu)
                            if ev_um[0] > 0:
                                t.wait_ge(dv.h, ev_um[0])
                            for ts2 in range(TSUB):
                                for k in range(KC):
                                    mm = t.matmul(psUm[:, ts2, :], wu_b[:, sl, k, :],
                                             xn[:, k, ts2 * 512:(ts2 + 1) * 512],
                                             start=(k == 0), stop=(k == KC - 1))
                            mm.then_inc(pe.h, 1)
                        t_um = pe.inc()
                        prev_u[sl] = t_um

                        @block.scalar
                        def _(s, sl=sl, t_gm=t_gm):
                            s.wait_ge(pe.h, t_gm)
                            if dv_wtd[0] > 0:
                                s.wait_ge(dv.h, dv_wtd[0])
                            s.activation(scrm[:],
                                         psGm.ap().rearrange("p a b -> p (a b)"),
                                         AF.Silu).then_inc(sc.h, 1)
                        sc_silu[0] = sc.inc()
                        ev_gm[0] = sc_silu[0]

                        @block.vector
                        def _(v, fb=fb, t_um=t_um, t_si=sc_silu[0], pdn=prev_down[0]):
                            v.wait_ge(pe.h, t_um)
                            v.wait_ge(sc.h, t_si)
                            if pdn > 0:
                                v.wait_ge(pe.h, pdn)
                            v.tensor_tensor(wtd[:, fb, :],
                                            psUm.ap().rearrange("p a b -> p (a b)"),
                                            scrm[:], OP.mult).then_inc(dv.h, 1)
                        dv_wtd[0] = dv.inc()
                        ev_um[0] = dv_wtd[0]
                        wtd_tags[fb] = dv_wtd[0]
                    wtd_ready = dv_wtd[0]

                    for hc in range(HC):
                        sl = hc % 2
                        wds = ws[4 + sl]
                        t_wd = planw.load(wds, wd_b[:, sl], wd_t[e, hc],
                                          war=[(pe, prev_d[sl])])
                        for tb in range(TB):
                            psl = [0, 2, 1, 3][(hc * TB + tb) % 4]

                            @block.tensor
                            def _(t, hc=hc, sl=sl, tb=tb, psl=psl, t_wd=t_wd,
                                  wtags=list(wtd_tags)):
                                if tb == 0:
                                    t.wait_ge(ws[4 + hc % 2].h, t_wd)
                                if ev_dm[psl] > 0:
                                    t.wait_ge(dv.h, ev_dm[psl])
                                first = (hc == 0 and tb == 0)
                                for k in range(FMC):
                                    if first:
                                        t.wait_ge(dv.h, wtags[k])
                                    mm = t.matmul(psDm[:, psl, :],
                                             wtd[:, k, tb * 128:(tb + 1) * 128],
                                             wd_b[:, sl, k, :],
                                             start=(k == 0), stop=(k == FMC - 1))
                                mm.then_inc(pe.h, 1)
                            t_dm = pe.inc()

                            @block.vector
                            def _(v, e=e, hc=hc, tb=tb, psl=psl, t_dm=t_dm,
                                  pg=prev_e_ev[0]):
                                v.wait_ge(pe.h, t_dm)
                                if hc == 0 and tb == 0 and pg > 0:
                                    v.wait_ge(dv.h, pg)
                                v.scalar_tensor_tensor(acc[:, tb, hc * 512:(hc + 1) * 512],
                                                       psDm[:, psl, :],
                                                       comb[:, tb, e:e + 1],
                                                       acc[:, tb, hc * 512:(hc + 1) * 512],
                                                       OP.mult, OP.add).then_inc(dv.h, 1)
                            ev_dm[psl] = dv.inc()
                        prev_d[sl] = pe.n
                    prev_down[0] = pe.n
                    prev_e_ev[0] = dv.n
                fence()
            if stop_after == "moe%d" % mt:
                finish()
                return nc

            with ExitStack() as ps:
                psS = ps.enter_context(nc.psum_tensor("psS%d" % mt, [128, 2, 512], F32))
                psU2 = ps.enter_context(nc.psum_tensor("psU2%d" % mt, [128, 2, 512], F32))
                psD2 = ps.enter_context(nc.psum_tensor("psD2%d" % mt, [128, 4, 512], F32))
                sg_bb = ps.enter_context(nc.sbuf_tensor("sg_bb%d" % mt, [128, 2, KC, CBW], BF16))
                su_bb = ps.enter_context(nc.sbuf_tensor("su_bb%d" % mt, [128, 2, KC, CBW], BF16))
                sd_bb = ps.enter_context(nc.sbuf_tensor("sd_bb%d" % mt, [128, 2, 2, D], BF16))
                y_sb = ps.enter_context(nc.sbuf_tensor("y_sb%d" % mt, [128, 2, 2, MT], BF16))
                scrs = ps.enter_context(nc.sbuf_tensor("scrs%d" % mt, [128, MT], F32))
                p0 = pe.n
                prev_sg = [p0, p0]; prev_su = [p0, p0]; prev_sd = [p0, p0]
                prev_y = [p0, p0]
                ev_s = [0]; ev_u2 = [0]; ev_d2 = [0, 0, 0, 0]
                sc_silu2 = [0]; dv_y = [0]; prev_p_ev = [0]
                y_tags = [[0, 0], [0, 0]]
                t_sdw_pair = [0, 0]
                for cb in range(NCB):
                    sl = cb % 2
                    t_sgw = planw.load(ws[0 + sl], sg_bb[:, sl], swg_t[cb],
                                       war=[(pe, prev_sg[sl])])
                    t_suw = planw.load(ws[2 + sl], su_bb[:, sl], swu_t[cb],
                                       war=[(pe, prev_su[sl])])
                    t_sdw_pair[sl] = planw.load(ws[4 + sl], sd_bb[:, sl], swd_t[cb],
                                                war=[(pe, prev_sd[sl])])
                    for ch in range(2):
                        @block.tensor
                        def _(t, sl=sl, ch=ch, t_sgw=t_sgw):
                            if ch == 0:
                                t.wait_ge(ws[0 + sl].h, t_sgw)
                            if ev_s[0] > 0:
                                t.wait_ge(sc.h, ev_s[0])
                            for ts2 in range(TSUB):
                                for k in range(KC):
                                    mm = t.matmul(psS[:, ts2, :],
                                             sg_bb[:, sl, k, ch * 128:(ch + 1) * 128],
                                             xn[:, k, ts2 * 512:(ts2 + 1) * 512],
                                             start=(k == 0), stop=(k == KC - 1))
                            mm.then_inc(pe.h, 1)
                        t_sm = pe.inc()
                        prev_sg[sl] = t_sm

                        @block.tensor
                        def _(t, sl=sl, ch=ch, t_suw=t_suw):
                            if ch == 0:
                                t.wait_ge(ws[2 + sl].h, t_suw)
                            if ev_u2[0] > 0:
                                t.wait_ge(dv.h, ev_u2[0])
                            for ts2 in range(TSUB):
                                for k in range(KC):
                                    mm = t.matmul(psU2[:, ts2, :],
                                             su_bb[:, sl, k, ch * 128:(ch + 1) * 128],
                                             xn[:, k, ts2 * 512:(ts2 + 1) * 512],
                                             start=(k == 0), stop=(k == KC - 1))
                            mm.then_inc(pe.h, 1)
                        t_um2 = pe.inc()
                        prev_su[sl] = t_um2

                        @block.scalar
                        def _(s, t_sm=t_sm):
                            s.wait_ge(pe.h, t_sm)
                            if dv_y[0] > 0:
                                s.wait_ge(dv.h, dv_y[0])
                            s.activation(scrs[:],
                                         psS.ap().rearrange("p a b -> p (a b)"),
                                         AF.Silu).then_inc(sc.h, 1)
                        sc_silu2[0] = sc.inc()
                        ev_s[0] = sc_silu2[0]

                        @block.vector
                        def _(v, sl=sl, ch=ch, t_um2=t_um2, t_si=sc_silu2[0],
                              pv=prev_y[sl]):
                            v.wait_ge(pe.h, t_um2)
                            v.wait_ge(sc.h, t_si)
                            if pv > 0 and ch == 0:
                                v.wait_ge(pe.h, pv)
                            v.tensor_tensor(y_sb[:, sl, ch, :],
                                            psU2.ap().rearrange("p a b -> p (a b)"),
                                            scrs[:], OP.mult).then_inc(dv.h, 1)
                        dv_y[0] = dv.inc()
                        ev_u2[0] = dv_y[0]
                        y_tags[sl][ch] = dv_y[0]

                    if cb % 2 == 1:
                        pair_ready = max(y_tags[0] + y_tags[1])
                        for tb in range(TB):
                            for fc in range(HC):
                                idx = tb * HC + fc
                                psl = idx % 2

                                @block.tensor
                                def _(t, tb=tb, fc=fc, psl=psl, idx=idx,
                                      pair_ready=pair_ready,
                                      t_sd0=t_sdw_pair[0], t_sd1=t_sdw_pair[1]):
                                    if idx == 0:
                                        t.wait_ge(ws[4].h, t_sd0)
                                        t.wait_ge(ws[5].h, t_sd1)
                                        t.wait_ge(dv.h, pair_ready)
                                    if ev_d2[psl] > 0:
                                        t.wait_ge(dv.h, ev_d2[psl])
                                    for k in range(4):
                                        slk, chk = (0, k) if k < 2 else (1, k - 2)
                                        mm = t.matmul(psD2[:, psl, :],
                                                 y_sb[:, slk, chk, tb * 128:(tb + 1) * 128],
                                                 sd_bb[:, slk, chk, fc * 512:(fc + 1) * 512],
                                                 start=(k == 0), stop=(k == 3))
                                    mm.then_inc(pe.h, 1)
                                t_d2 = pe.inc()

                                @block.vector
                                def _(v, tb=tb, fc=fc, psl=psl, t_d2=t_d2,
                                      idx=idx, pg=prev_p_ev[0]):
                                    v.wait_ge(pe.h, t_d2)
                                    if idx == 0 and pg > 0:
                                        v.wait_ge(dv.h, pg)
                                    v.scalar_tensor_tensor(acc[:, tb, fc * 512:(fc + 1) * 512],
                                                           psD2[:, psl, :],
                                                           gt[:, tb, 0:1],
                                                           acc[:, tb, fc * 512:(fc + 1) * 512],
                                                           OP.mult, OP.add).then_inc(dv.h, 1)
                                ev_d2[psl] = dv.inc()
                        prev_p_ev[0] = dv.n
                        prev_y = [pe.n, pe.n]
                        prev_sd = [pe.n, pe.n]
                fence()
            if stop_after == "shared%d" % mt:
                finish()
                return nc

            last_ev = dv.n
            t_st = planx.load(
                qs, out[tok0:tok0 + MT, :].rearrange("(a p) d -> p a d", p=128),
                acc[:], war=[(dv, last_ev), (qs, qs.n - 16)])
            store_tags.append(t_st)

        finish()

    return nc



def _tile_lhsT_cols(wT, nblk, blkw):
    K = wT.shape[0]
    kc = K // 128
    a = wT.reshape(kc, 128, nblk, blkw)
    return np.ascontiguousarray(a.transpose(2, 1, 0, 3))


def _tile_rhs_rows(w, nblk, kper, nw):
    a = w.reshape(nblk, kper, 128, nw)
    return np.ascontiguousarray(a.transpose(0, 2, 1, 3))


_NC_CACHE = {}


def kernel(L0_attn_x, L0_attn_Wq, L0_attn_Wk, L0_attn_Wv, L0_attn_Wo,
           L0_ffn_router_weight, L0_ffn_w_gate, L0_ffn_w_up, L0_ffn_w_down,
           L0_ffn_sw_gate, L0_ffn_sw_up, L0_ffn_sw_down, L0_ffn_sgate_w):
    x = np.ascontiguousarray(np.asarray(L0_attn_x, dtype=np.float32))
    Wq = np.asarray(L0_attn_Wq, dtype=np.float32)
    Wk = np.asarray(L0_attn_Wk, dtype=np.float32)
    Wv = np.asarray(L0_attn_Wv, dtype=np.float32)
    Wo = np.asarray(L0_attn_Wo, dtype=np.float32)
    rw = np.asarray(L0_ffn_router_weight, dtype=np.float32)
    wg = np.asarray(L0_ffn_w_gate, dtype=np.float32)
    wu = np.asarray(L0_ffn_w_up, dtype=np.float32)
    wd = np.asarray(L0_ffn_w_down, dtype=np.float32)
    swg = np.asarray(L0_ffn_sw_gate, dtype=np.float32)
    swu = np.asarray(L0_ffn_sw_up, dtype=np.float32)
    swd = np.asarray(L0_ffn_sw_down, dtype=np.float32)
    sgw = np.asarray(L0_ffn_sgate_w, dtype=np.float32)

    feeds = dict(
        wq_t=_tile_lhsT_cols(np.ascontiguousarray(Wq.T), KC, 128),
        wk_t=_tile_lhsT_cols(np.ascontiguousarray(Wk.T), KC, 128),
        wv_t=_tile_lhsT_cols(np.ascontiguousarray(Wv.T), KC, 128),
        wo_t=_tile_rhs_rows(np.ascontiguousarray(Wo.T), 4, 4, D),
        rw_t=_tile_lhsT_cols(np.ascontiguousarray(rw.T), 1, E)[0],
        sgw_t=_tile_lhsT_cols(np.ascontiguousarray(sgw), 1, 1)[0],
        wg_t=np.stack([_tile_lhsT_cols(wg[e], FMC, 128) for e in range(E)]),
        wu_t=np.stack([_tile_lhsT_cols(wu[e], FMC, 128) for e in range(E)]),
        wd_t=np.stack([np.ascontiguousarray(
            wd[e].reshape(FMC, 128, HC, 512).transpose(2, 1, 0, 3)) for e in range(E)]),
        swg_t=_tile_lhsT_cols(swg, NCB, CBW),
        swu_t=_tile_lhsT_cols(swu, NCB, CBW),
        swd_t=_tile_rhs_rows(swd, NCB, 2, D),
        ident=np.eye(128, dtype=ml_dtypes.bfloat16),
        identf=np.eye(128, dtype=np.float32),
    )

    if "nc" not in _NC_CACHE:
        _NC_CACHE["nc"] = build_nc()
    nc = _NC_CACHE["nc"]

    in_maps = []
    for c in range(8):
        b, h = c // 2, c % 2
        m = dict(feeds)
        m["x_own"] = np.ascontiguousarray(x[b, h * TCORE:(h + 1) * TCORE, :])
        m["x_pre"] = np.ascontiguousarray(x[b, 0:TCORE, :])
        m["flag"] = np.full((128, 1), float(h), dtype=np.float32)
        in_maps.append(m)

    res = run_bass_kernel_spmd(nc, in_maps, core_ids=list(range(8)))
    full = np.empty((4, 4096, D), dtype=np.float32)
    for c in range(8):
        b, h = c // 2, c % 2
        full[b, h * TCORE:(h + 1) * TCORE, :] = np.asarray(res.results[c]["out"])
    return full
